# version 01
# baseline (speedup 1.0000x reference)
import numpy as np
import ml_dtypes
import concourse.bass as bass
import concourse.mybir as mybir
from concourse.bass_utils import run_bass_kernel_spmd
from contextlib import ExitStack

F32 = mybir.dt.float32
BF16 = mybir.dt.bfloat16
AF = mybir.ActivationFunctionType
ALU = mybir.AluOpType
AX = mybir.AxisListType
BF = ml_dtypes.bfloat16

NCORE = 8
SEQ = 2
N = 2048
D = 2048
R = SEQ * N
TT = R // 128
TTS = N // 128
NH_A = 16
HD_A = 128
NQ = 4
NH_B = 8
DH_B = 64
DKV = 512
NE = 8
NR = 16

ALL_STAGES = ("ln1", "dmat1", "attna0", "attna1", "ln2", "dmat2", "ffn1",
              "ln3", "dmat3", "attnb0", "attnb1", "ln4", "dmat4", "ffn2")

A_GROUPS = 4
A_HALVES = 2
A_OG = True
A_DELTA = True
A_DCHUNKS = 8
A_PROJLVL = 3
F_LORA = True


class SemC:
    __slots__ = ("h", "n", "name")
    def __init__(self, h, name):
        self.h, self.n, self.name = h, 0, name


class _Op:
    __slots__ = ("kind", "meth", "args", "kwargs", "inc")
    def __init__(self, kind, meth, args, kwargs, inc):
        self.kind, self.meth, self.args, self.kwargs, self.inc = kind, meth, args, kwargs, inc


class Eng:
    def __init__(self, kb, key):
        self.kb, self.key = kb, key

    def op(self, meth, *args, inc=None, **kwargs):
        if inc is not None:
            inc[0].n += inc[1]
        self.kb.plan[self.key].append(_Op("op", meth, args, kwargs, inc))

    def wait(self, sem, val=None):
        v = sem.n if val is None else val
        if v > 0:
            self.kb.plan[self.key].append(_Op("wait", None, (sem, v), {}, None))

    def dma(self, out, in_, inc, **kw):
        self.op("dma_start", out=out, in_=in_, inc=(inc, 16), **kw)

    def drain(self, inc=None):
        self.op("drain", inc=inc)

    def inc_last(self, sem, k=1):
        ops = self.kb.plan[self.key]
        assert ops and ops[-1].kind == "op" and ops[-1].inc is None, "inc_last misuse"
        sem.n += k
        ops[-1].inc = (sem, k)


class Arena:
    def __init__(self, tensor, cols):
        self.t, self.cols, self.top = tensor, cols, 0
        self.marks = []

    def alloc(self, n):
        a = self.top
        assert a + n <= self.cols, f"arena overflow: {a}+{n} > {self.cols}"
        self.top = a + n
        return self.t[:, a:a + n]

    def alloc3(self, n, w):
        return self.alloc(n * w).rearrange("p (n w) -> p n w", w=w)

    def push(self):
        self.marks.append(self.top)

    def pop(self):
        self.top = self.marks.pop()


class Kb:
    def __init__(self, nc):
        self.nc = nc
        self.plan = {k: [] for k in "TVSYG"}
        self.T = Eng(self, "T"); self.V = Eng(self, "V"); self.S = Eng(self, "S")
        self.Y = Eng(self, "Y"); self.G = Eng(self, "G")
        self.sems = []
        self.ctx = None

    def sem(self, name):
        s = SemC(self.ctx.enter_context(self.nc.semaphore(name)), name)
        self.sems.append(s)
        return s

    def sbuf(self, name, shape, dtype=F32):
        return self.ctx.enter_context(self.nc.sbuf_tensor(name, shape, dtype))

    def psum(self, name, shape, dtype=F32):
        return self.ctx.enter_context(self.nc.psum_tensor(name, shape, dtype))

    def emit(self):
        block = self.ctx.enter_context(self.nc.Block())

        def mk(key):
            def body(eng):
                for o in self.plan[key]:
                    if o.kind == "wait":
                        sem, v = o.args
                        eng.wait_ge(sem.h, v)
                    else:
                        ins = getattr(eng, o.meth)(*o.args, **o.kwargs)
                        if o.inc is not None:
                            ins.then_inc(o.inc[0].h, o.inc[1])
            return body
        block.tensor(mk("T"))
        block.vector(mk("V"))
        block.scalar(mk("S"))
        block.sync(mk("Y"))
        block.gpsimd(mk("G"))


def host_inputs(inputs, core):
    x = np.asarray(inputs["L0_attn_a_x"])
    t = lambda a: np.ascontiguousarray(np.asarray(a).T)
    return {
        "x": np.ascontiguousarray(x[2 * core: 2 * core + 2].reshape(R, D)),
        "wqaT": t(inputs["L0_attn_a_Wq"]),
        "wkaT": t(inputs["L0_attn_a_Wk"]),
        "wvaT": t(inputs["L0_attn_a_Wv"]),
        "wbetaT": t(inputs["L0_attn_a_Wbeta"]),
        "wg1T": t(inputs["L0_attn_a_Wg1"]),
        "wg2T": t(inputs["L0_attn_a_Wg2"]),
        "gnorm": np.ascontiguousarray(np.asarray(inputs["L0_attn_a_g_norm"]).reshape(1, D)),
        "woaT": t(inputs["L0_attn_a_Wo"]),
        "wqbT": t(inputs["L0_attn_b_Wq"]),
        "wkbT": t(inputs["L0_attn_b_Wk"]),
        "wvbT": t(inputs["L0_attn_b_Wv"]),
        "mem": np.ascontiguousarray(np.asarray(inputs["L0_attn_b_memory"])),
        "memnorm": np.ascontiguousarray(np.asarray(inputs["L0_attn_b_memory_norm"])),
        "memw": np.ascontiguousarray(np.asarray(inputs["L0_attn_b_memory_weight"]).reshape(1, 1)),
        "wfT": t(inputs["L0_ffn_W"]),
        "fb": np.ascontiguousarray(np.asarray(inputs["L0_ffn_b"]).reshape(1, D)),
        "aTall": np.ascontiguousarray(
            np.asarray(inputs["L0_ffn_lora_A"]).transpose(2, 0, 1).reshape(D, NE * NR)),
        "ball": np.ascontiguousarray(
            np.asarray(inputs["L0_ffn_lora_B"]).transpose(0, 2, 1).reshape(NE * NR, D)),
        "gwT": t(inputs["L0_ffn_gate_w"]),
        "gb": np.ascontiguousarray(np.asarray(inputs["L0_ffn_gate_b"]).reshape(1, NE)),
        "idb": np.eye(128, dtype=np.float32).astype(BF),
        "mask_l": np.tril(np.ones((128, 128), np.float32), -1).astype(BF),
        "mask_u": np.triu(np.ones((128, 128), np.float32), 1).astype(BF),
        "mask_ui": np.triu(np.ones((128, 128), np.float32), 0).astype(BF),
        "onesb": np.ones((1, 512), np.float32).astype(BF),
        "onesf": np.ones((1, 512), np.float32),
        "ccols": np.tile(np.array([[1e-5, 1e-6, -16.0]], np.float32), (128, 1)),
    }


IN_SHAPES = {
    "x": ((R, D), F32), "wqaT": ((D, D), F32), "wkaT": ((D, D), F32),
    "wvaT": ((D, D), F32), "wbetaT": ((D, NH_A), F32), "wg1T": ((D, 128), F32),
    "wg2T": ((128, D), F32), "gnorm": ((1, D), F32), "woaT": ((D, D), F32),
    "wqbT": ((D, D), F32), "wkbT": ((D, DKV), F32), "wvbT": ((D, DKV), F32),
    "mem": ((DKV, DKV), F32), "memnorm": ((DKV, 1), F32), "memw": ((1, 1), F32),
    "wfT": ((D, D), F32), "fb": ((1, D), F32), "aTall": ((D, NE * NR), F32),
    "ball": ((NE * NR, D), F32), "gwT": ((D, NE), F32), "gb": ((1, NE), F32),
    "idb": ((128, 128), BF16),
    "mask_l": ((128, 128), BF16), "mask_u": ((128, 128), BF16),
    "mask_ui": ((128, 128), BF16), "onesb": ((1, 512), BF16), "onesf": ((1, 512), F32),
    "ccols": ((128, 3), F32),
}

SCRATCH = {
    "xn_tm": ((R, D), BF16),
    "xn_fm": ((16, 128, R), BF16),
    "xres_a": ((R, D), F32),
    "xres_b": ((R, D), F32),
    "o_scr": ((NH_A, N, HD_A), BF16),
    "og_tm": ((N, D), BF16),
    "og_fm": ((16, 128, N), BF16),
    "adot_scr": ((D, N), BF16),
    "amem_scr": ((D, N), BF16),
    "sums_scr": ((NQ * NH_B, N), BF16),
    "dens_scr": ((NQ, N), BF16),
    "dbg_scr": ((32, 128, 512), BF16),
}


def build(debug_outs=(), stages=ALL_STAGES):
    stages = set(stages)
    nc = bass.Bass("TRN2", target_bir_lowering=False)
    kb = Kb(nc)
    with ExitStack() as ctx:
        kb.ctx = ctx
        pr_debug = set(debug_outs)
        dr = {}
        for k, (shape, dt) in IN_SHAPES.items():
            dr[k] = nc.dram_tensor(k, list(shape), dt, kind="ExternalInput")
        dr["out"] = nc.dram_tensor("out", [R, D], F32, kind="ExternalOutput")
        for k, (shape, dt) in SCRATCH.items():
            kind = "ExternalOutput" if k in debug_outs else "Internal"
            dr[k] = nc.dram_tensor(k, list(shape), dt, kind=kind)
        T, V, S, Y, G = kb.T, kb.V, kb.S, kb.Y, kb.G

        c_id = kb.sbuf("c_id", [128, 128], BF16)
        c_ml = kb.sbuf("c_ml", [128, 128], BF16)
        c_mu = kb.sbuf("c_mu", [128, 128], BF16)
        c_mui = kb.sbuf("c_mui", [128, 128], BF16)
        c_ones = kb.sbuf("c_ones", [1, 512], BF16)
        c_onesf = kb.sbuf("c_onesf", [1, 512], F32)
        c_cc = kb.sbuf("c_cc", [128, 3], F32)
        nc.const_aps.register(F32, 1e-5, c_cc[:, 0:1])
        nc.const_aps.register(F32, 1e-6, c_cc[:, 1:2])
        nc.const_aps.register(F32, -16.0, c_cc[:, 2:3])
        fbrow = kb.sbuf("fbrow", [1, D], BF16)
        gbrow = kb.sbuf("gbrow", [1, NE], BF16)
        gnrow = kb.sbuf("gnrow", [1, D], F32)
        memwc = kb.sbuf("memwc", [1, 1], F32)
        wcol = kb.sbuf("wcol", [128, 2], F32)
        colA = kb.sbuf("colA", [128, 8], F32)
        colB = kb.sbuf("colB", [128, 8], F32)
        colC = kb.sbuf("colC", [128, 32], F32)
        gbcast = kb.sbuf("gbcast", [128, D], F32)
        swbeta = kb.sbuf("swbeta", [128, 16, NH_A], BF16)
        swg1 = kb.sbuf("swg1", [128, 16, 128], BF16)
        swg2 = kb.sbuf("swg2", [128, D], BF16)
        sgw = kb.sbuf("sgw", [128, 16, NE], BF16)
        saT = kb.sbuf("saT", [128, 16, NE * NR], BF16)
        sball = kb.sbuf("sball", [128, D], BF16)
        smem = kb.sbuf("smem", [128, 4, DKV], BF16)
        smn = kb.sbuf("smn", [128, 4, 1], BF16)

        ABF_COLS = 55000
        AF32_COLS = 12500
        abf = Arena(kb.sbuf("abf", [128, ABF_COLS], BF16), ABF_COLS)
        af32 = Arena(kb.sbuf("af32", [128, AF32_COLS], F32), AF32_COLS)
        xt = [af32.alloc(D) for _ in range(2)]
        sqs = af32.alloc(D)
        xnew = [af32.alloc(D)]
        xnt = [abf.alloc(D) for _ in range(2)]

        pA = kb.psum("pA", [128, 512], F32)
        pB = kb.psum("pB", [128, 512], F32)
        pC = kb.psum("pC", [128, 512], F32)
        pD = kb.psum("pD", [128, 512], F32)
        pE = kb.psum("pE", [128, 512], F32)
        pF = kb.psum("pF", [128, 512], F32)
        pTa = kb.psum("pTa", [128, 1024], BF16)
        pTb = kb.psum("pTb", [128, 1024], BF16)
        PSF = [pA, pB, pC, pD, pE, pF]

        s_ld = kb.sem("s_ld")
        s_gw = kb.sem("s_gw")
        s_t = kb.sem("s_t")
        s_v = kb.sem("s_v")
        s_s = kb.sem("s_s")
        s_xt = [kb.sem("s_xt0"), kb.sem("s_xt1")]
        s_xn = [kb.sem("s_xn0"), kb.sem("s_xn1")]
        s_xw = [kb.sem("s_xw0"), kb.sem("s_xw1")]
        s_fm = [kb.sem("s_fm0"), kb.sem("s_fm1")]
        s_fs = [kb.sem("s_fs0"), kb.sem("s_fs1")]
        s_sa = [kb.sem("s_sa0"), kb.sem("s_sa1")]
        s_sb = [kb.sem("s_sb0"), kb.sem("s_sb1")]
        s_os = [kb.sem("s_os0"), kb.sem("s_os1")]
        s_pe = kb.sem("s_pe")
        s_ev = kb.sem("s_ev")
        s_dbg = kb.sem("s_dbg")

        def dbg(row, ap, sem):
            if "dbg_scr" not in pr_debug:
                return
            G.wait(sem)
            wdt = ap.shape[-1] if len(ap.shape) == 2 else 512
            npp = ap.shape[0]
            G.dma(dr["dbg_scr"][row, 0:npp, 0:wdt], ap, s_dbg)

        def barrier():
            for e in (T, V, S, Y, G):
                for s in kb.sems:
                    e.wait(s)

        def prologue():
            Y.dma(c_id[:], dr["idb"][:, :], s_ld)
            Y.dma(c_ml[:], dr["mask_l"][:, :], s_ld)
            Y.dma(c_mu[:], dr["mask_u"][:, :], s_ld)
            Y.dma(c_mui[:], dr["mask_ui"][:, :], s_ld)
            Y.dma(c_ones[:], dr["onesb"][:, :], s_ld)
            Y.dma(c_onesf[:], dr["onesf"][:, :], s_ld)
            Y.dma(c_cc[:], dr["ccols"][:, :], s_ld)
            Y.dma(gnrow[:], dr["gnorm"][:, :], s_ld)
            Y.dma(memwc[:], dr["memw"][:, :], s_ld)
            G.dma(fbrow[:], dr["fb"][:, :], s_gw)
            G.dma(gbrow[:], dr["gb"][:, :], s_gw)
            G.dma(swbeta[:, :, :], dr["wbetaT"][:, :].rearrange("(c p) h -> p c h", p=128), s_gw)
            G.dma(swg1[:, :, :], dr["wg1T"][:, :].rearrange("(c p) g -> p c g", p=128), s_gw)
            G.dma(swg2[:, :], dr["wg2T"][:, :], s_gw)
            G.dma(sgw[:, :, :], dr["gwT"][:, :].rearrange("(c p) e -> p c e", p=128), s_gw)
            G.dma(saT[:, :, :], dr["aTall"][:, :].rearrange("(c p) e -> p c e", p=128), s_gw)
            G.dma(sball[:, :], dr["ball"][:, :], s_gw)
            G.dma(smem[:, :, :], dr["mem"][:, :].rearrange("(c p) m -> p c m", p=128), s_gw)
            G.dma(smn[:, :, :], dr["memnorm"][:, :].rearrange("(c p) m -> p c m", p=128), s_gw,
                  allow_slow_non_contiguous=True)
            T.wait(s_ld)
            for j in range(4):
                T.op("matmul", pA[:, :], c_onesf[0:1, 0:128],
                     gnrow[0:1, 512 * j:512 * j + 512], start=True, stop=True,
                     inc=(s_t, 1))
                V.wait(s_t)
                V.op("tensor_copy", gbcast[:, 512 * j:512 * j + 512], pA[:, :])
                V.drain(inc=(s_v, 1))
                T.wait(s_v)
            S.wait(s_ld)
            S.op("activation", colA[0:1, 0:1], memwc[0:1, 0:1], AF.Sigmoid)
            S.drain(inc=(s_s, 1))
            T.wait(s_s)
            T.op("matmul", pB[:, 0:1], c_onesf[0:1, 0:128], colA[0:1, 0:1],
                 start=True, stop=True, inc=(s_t, 1))
            V.wait(s_t)
            V.op("tensor_copy", wcol[:, 0:1], pB[:, 0:1])
            V.drain()
            V.op("tensor_scalar", wcol[:, 1:2], wcol[:, 0:1], -1.0, 1.0, ALU.mult, ALU.add)
            V.drain(inc=(s_v, 1))
            T.wait(s_v)

        lnm = {}

        def ln_tile(i, src_dram, b):
            S.wait(s_s, lnm.get(i - 2, 0))
            Y.dma(xt[b], src_dram[128 * i:128 * i + 128, :], s_xt[b])
            V.wait(s_xt[b])
            V.op("scalar_tensor_tensor", sqs, xt[b], 1.0, xt[b],
                 ALU.mult, ALU.mult, accum_out=colA[:, 1:2])
            V.op("tensor_reduce", colA[:, 0:1], xt[b], AX.X, ALU.add)
            V.drain()
            V.op("tensor_scalar", colA[:, 2:3], colA[:, 0:1], 1.0 / D, None, ALU.mult)
            V.drain()
            V.op("tensor_tensor", colA[:, 3:4], colA[:, 2:3], colA[:, 2:3], ALU.mult)
            V.drain()
            V.op("scalar_tensor_tensor", colA[:, 5:6], colA[:, 1:2], 1.0 / D,
                 colA[:, 3:4], ALU.mult, ALU.subtract)
            V.drain(inc=(s_v, 1))
            S.wait(s_v)
            S.op("activation", colB[:, 0:1], colA[:, 5:6], AF.Ln, bias=1e-5)
            S.drain()
            S.op("activation", colB[:, 1:2], colB[:, 0:1], AF.Exp, scale=-0.5)
            S.drain(inc=(s_s, 1))
            V.wait(s_s)
            V.op("scalar_tensor_tensor", colB[:, 2:3], colA[:, 2:3], -1.0,
                 colB[:, 1:2], ALU.mult, ALU.mult)
            V.drain(inc=(s_v, 1))
            S.wait(s_v)
            S.wait(s_xn[b])
            S.op("activation", xnt[b], xt[b], AF.Identity,
                 bias=colB[:, 2:3], scale=colB[:, 1:2])
            S.drain(inc=(s_s, 1))
            lnm[i] = s_s.n
            Y.wait(s_s)
            Y.dma(dr["xn_tm"][128 * i:128 * i + 128, :], xnt[b], s_xn[b])

        def ln_pass(src_dram):
            lnm.clear()
            for i in range(TT):
                ln_tile(i, src_dram, i % 2)
            Y.wait(s_xn[0]); Y.wait(s_xn[1])

        def dmat_pass(src="xn_tm", dst="xn_fm", rows=R):
            abf.push()
            fmb = abf.alloc(rows)
            for c in range(16):
                Y.wait(s_fs[0])
                Y.op("dma_start", out=fmb,
                     in_=dr[src][0:rows, 128 * c:128 * c + 128],
                     transpose=True, inc=(s_fm[0], 16))
                Y.wait(s_fm[0])
                Y.dma(dr[dst][c, :, 0:rows], fmb, s_fs[0])
            Y.wait(s_fs[0])
            abf.pop()

        mk = {}

        def attna_seq(seq, src_res):
            ts0 = seq * N
            abf.push(); af32.push()
            sxnt = abf.alloc3(16, 128)
            swk = abf.alloc3(16, 512)
            swv = abf.alloc3(16, 512)
            swq = abf.alloc3(16, 512)
            ktm = abf.alloc3(8, 512)
            vtm = abf.alloc3(8, 512)
            qtm = abf.alloc3(8, 512)
            sbeta = af32.alloc(16 * NH_A).rearrange("p (t h) -> p t h", h=NH_A)
            kq = abf.alloc(512)
            sqk = sqs[:, 0:512]
            uP = [af32.alloc(512) for _ in range(2)]
            nP = [af32.alloc(512) for _ in range(2)]
            kfm = abf.alloc(512)
            ktfm = abf.alloc(512)
            qfm = abf.alloc(512)
            m2t = abf.alloc(512)
            ktil = abf.alloc(512)
            xping = [af32.alloc(512) for _ in range(2)]
            dbf = abf.alloc(512)
            sf32 = af32.alloc(512)
            sbf = abf.alloc(512)
            osg = [abf.alloc(512) for _ in range(2)]

            def proj_tile(g, lt):
                sl = lt % 8
                Y.wait(s_t, mk.get(("asx", seq, g, lt - 1), 0))
                Y.dma(sxnt[:, :, :],
                      dr["xn_fm"][:, :, ts0 + 128 * lt: ts0 + 128 * lt + 128]
                      .rearrange("c p t -> p c t"), s_sa[0])
                T.wait(s_sa[0])
                T.wait(s_ev, mk.get(("akv", seq, g, lt - 1), 0))
                for c in range(16):
                    st, sp = (c == 0), (c == 15)
                    T.op("matmul", pA[:, :], sxnt[:, c, :], swk[:, c, :], start=st, stop=sp)
                    T.op("matmul", pB[:, :], sxnt[:, c, :], swv[:, c, :], start=st, stop=sp)
                    T.op("matmul", pC[:, :], sxnt[:, c, :], swq[:, c, :], start=st, stop=sp)
                    if g == 0:
                        T.op("matmul", pD[:, 0:NH_A], sxnt[:, c, :],
                             swbeta[:, c, :], start=st, stop=sp)
                T.inc_last(s_t)
                mk[("asx", seq, g, lt)] = s_t.n
                S.wait(s_t)
                if A_PROJLVL >= 2:
                    S.op("activation", kq, pA[:, :], AF.Silu)
                    S.op("copy", vtm[:, sl, :], pB[:, :])
                    S.op("activation", qtm[:, sl, :], pC[:, :], AF.Silu)
                    if g == 0:
                        S.op("activation", colC[:, 16:16 + NH_A], pD[:, 0:NH_A], AF.Sigmoid)
                S.drain(inc=(s_s, 1))
                if A_PROJLVL < 3:
                    V.wait(s_s)
                    V.drain(inc=(s_ev, 1))
                    mk[("akv", seq, g, lt)] = s_ev.n
                    return
                if g == 0:
                    V.wait(s_s)
                    V.op("tensor_scalar", sbeta[:, lt, :], colC[:, 16:16 + NH_A],
                         2.0, None, ALU.mult)
                    V.drain()
                V.wait(s_s)
                V.op("tensor_tensor", sqk, kq, kq, ALU.mult)
                V.drain()
                V.op("tensor_reduce", colC[:, 0:4].unsqueeze(2),
                     sqk.rearrange("p (h d) -> p h d", d=128), AX.X, ALU.add)
                V.drain(inc=(s_v, 1))
                S.wait(s_v)
                S.op("activation", colC[:, 4:8], colC[:, 0:4], AF.Ln, bias=1e-6)
                S.drain()
                S.op("activation", colC[:, 8:12], colC[:, 4:8], AF.Exp, scale=-0.5)
                S.drain(inc=(s_s, 1))
                V.wait(s_s)
                V.op("tensor_tensor", ktm[:, sl, :].rearrange("p (h d) -> p h d", d=128),
                     kq.rearrange("p (h d) -> p h d", d=128),
                     colC[:, 8:12].unsqueeze(2).broadcast_to([128, 4, 128]),
                     ALU.mult)
                V.drain(inc=(s_ev, 1))
                mk[("akv", seq, g, lt)] = s_ev.n

            for g in range(A_GROUPS):
                G.wait(s_t)
                G.dma(swk[:, :, :], dr["wkaT"][:, 512 * g:512 * g + 512]
                      .rearrange("(c p) o -> p c o", p=128), s_gw)
                G.dma(swv[:, :, :], dr["wvaT"][:, 512 * g:512 * g + 512]
                      .rearrange("(c p) o -> p c o", p=128), s_gw)
                G.dma(swq[:, :, :], dr["wqaT"][:, 512 * g:512 * g + 512]
                      .rearrange("(c p) o -> p c o", p=128), s_gw)
                T.wait(s_gw)
                for half in range(A_HALVES):
                    T.wait(s_v); T.wait(s_s)
                    for lt in range(8 * half, 8 * half + 8):
                        proj_tile(g, lt)
                    T.wait(s_ev); V.wait(s_ev); S.wait(s_ev)
                    if A_DELTA:
                        delta_group(seq, g, half, ktm, vtm, qtm, sbeta, uP, nP, kfm,
                                    ktfm, qfm, m2t, ktil, xping, dbf, sf32, sbf, osg)
                barrier()
            abf.pop(); af32.pop()
            if A_OG:
                og_phase(seq, src_res)

        def delta_group(seq, g, half, ktm, vtm, qtm, sbeta, uP, nP, kfm, ktfm, qfm,
                        m2t, ktil, xping, dbf, sf32, sbf, osg):
            q3 = lambda ap: ap.rearrange("p (h d) -> p h d", d=128)
            mu_b = c_mu[:, :].unsqueeze(1).broadcast_to([128, 4, 128])
            ml_b = c_ml[:, :].unsqueeze(1).broadcast_to([128, 4, 128])
            mui_b = c_mui[:, :].unsqueeze(1).broadcast_to([128, 4, 128])
            for c in range(8 * half, 8 * half + A_DCHUNKS):
                sl = c % 8
                KT = ktm[:, sl, :]
                VT = vtm[:, sl, :]
                QT = qtm[:, sl, :]
                bsl = sbeta[:, c, 4 * g:4 * g + 4].unsqueeze(2).broadcast_to([128, 4, 128])
                V.wait(s_t, mk.get(("dend", seq, g, c - 1), 0))
                V.op("tensor_tensor", q3(ktil), q3(KT), bsl, ALU.mult)
                V.drain(inc=(s_v, 1))
                T.wait(s_v)
                T.wait(s_s, mk.get(("dtp", seq, g, c - 1), 0))
                for h in range(4):
                    hs = slice(128 * h, 128 * h + 128)
                    T.op("transpose", pTa[:, hs], KT[:, hs], c_id[:, :])
                    T.op("transpose", pTa[:, 512 + 128 * h:640 + 128 * h],
                         ktil[:, hs], c_id[:, :])
                    T.op("transpose", pTb[:, hs], QT[:, hs], c_id[:, :])
                T.inc_last(s_t)
                S.wait(s_t)
                S.wait(s_t, mk.get(("dend", seq, g, c - 1), 0))
                S.op("copy", kfm, pTa[:, 0:512])
                S.op("copy", ktfm, pTa[:, 512:1024])
                S.op("copy", qfm, pTb[:, 0:512])
                S.drain(inc=(s_s, 1))
                mk[("dtp", seq, g, c)] = s_s.n
                if c == 0 and g == 0 and seq == 0:
                    dbg(0, KT, s_ev); dbg(1, VT, s_ev); dbg(2, QT, s_ev)
                    dbg(3, ktil, s_v); dbg(4, kfm, s_s); dbg(5, ktfm, s_s); dbg(6, qfm, s_s)
                T.wait(s_s)
                T.wait(s_v, mk.get(("dmsk", seq, g, c - 1), 0))
                for h in range(4):
                    hs = slice(128 * h, 128 * h + 128)
                    T.op("matmul", pA[:, hs], kfm[:, hs], ktfm[:, hs], start=True, stop=True)
                    T.op("matmul", pB[:, hs], ktfm[:, hs], kfm[:, hs], start=True, stop=True)
                    T.op("matmul", pC[:, hs], kfm[:, hs], qfm[:, hs], start=True, stop=True)
                T.inc_last(s_t)
                V.wait(s_t)
                V.op("tensor_tensor", q3(uP[0]), q3(pA[:, :]), mu_b, ALU.mult)
                V.op("tensor_tensor", q3(nP[0]), q3(pB[:, :]), ml_b, ALU.mult)
                V.op("tensor_tensor", q3(m2t), q3(pC[:, :]), mui_b, ALU.mult)
                V.drain(inc=(s_v, 1))
                mk[("dmsk", seq, g, c)] = s_v.n
                if c == 0 and g == 0 and seq == 0:
                    dbg(7, uP[0], s_v); dbg(8, nP[0], s_v); dbg(9, m2t, s_v)

                if c == 0:
                    V.op("tensor_tensor", q3(xping[0]), q3(VT), bsl, ALU.mult)
                    V.drain(inc=(s_v, 1))
                else:
                    T.wait(s_ev)
                    T.wait(s_s, mk[("doev", seq, g, c - 1)] if
                           ("doev", seq, g, c - 1) in mk else 0)
                    for h in range(4):
                        hs = slice(128 * h, 128 * h + 128)
                        T.op("matmul", pF[:, hs], kfm[:, hs], sbf[:, hs],
                             start=True, stop=True)
                    T.inc_last(s_t)
                    V.wait(s_t)
                    V.op("tensor_tensor", sqs[:, 0:512], VT, pF[:, :], ALU.subtract)
                    V.drain()
                    V.op("tensor_tensor", q3(xping[0]),
                         sqs[:, 0:512].rearrange("p (h d) -> p h d", d=128),
                         bsl, ALU.mult)
                    V.drain(inc=(s_v, 1))
                if c == 0 and g == 0 and seq == 0:
                    dbg(10, xping[0], s_v)
                cur = 0
                xcur = 0
                T.wait(s_v)
                for h in range(4):
                    hs = slice(128 * h, 128 * h + 128)
                    T.op("matmul", pD[:, hs], uP[0][:, hs], xping[0][:, hs],
                         start=True, stop=True)
                T.inc_last(s_t)
                V.wait(s_t)
                V.op("tensor_tensor", xping[1], xping[0], pD[:, :], ALU.subtract)
                V.drain(inc=(s_v, 1))
                xcur = 1
                for lv in (2, 4, 8, 16, 32, 64):
                    T.wait(s_v)
                    T.wait(s_s)
                    for h in range(4):
                        hs = slice(128 * h, 128 * h + 128)
                        T.op("matmul", pD[:, hs], nP[cur][:, hs], uP[cur][:, hs],
                             start=True, stop=True)
                        if lv < 64:
                            T.op("matmul", pE[:, hs], uP[cur][:, hs], nP[cur][:, hs],
                                 start=True, stop=True)
                    T.inc_last(s_t)
                    S.wait(s_t)
                    S.op("copy", uP[1 - cur], pD[:, :])
                    if lv < 64:
                        S.op("copy", nP[1 - cur], pE[:, :])
                    S.drain(inc=(s_s, 1))
                    cur = 1 - cur
                    T.wait(s_s)
                    for h in range(4):
                        hs = slice(128 * h, 128 * h + 128)
                        T.op("matmul", pD[:, hs], uP[cur][:, hs], xping[xcur][:, hs],
                             start=True, stop=True)
                    T.inc_last(s_t)
                    V.wait(s_t)
                    V.op("tensor_tensor", xping[1 - xcur], xping[xcur], pD[:, :],
                         ALU.add)
                    V.drain(inc=(s_v, 1))
                    xcur = 1 - xcur
                delta = xping[xcur]
                V.op("tensor_copy", dbf, delta)
                V.drain(inc=(s_v, 1))
                T.wait(s_v)
                for h in range(4):
                    hs = slice(128 * h, 128 * h + 128)
                    if c > 0:
                        T.op("matmul", pF[:, hs], qfm[:, hs], sbf[:, hs],
                             start=True, stop=False)
                        T.op("matmul", pF[:, hs], m2t[:, hs], dbf[:, hs],
                             start=False, stop=True)
                    else:
                        T.op("matmul", pF[:, hs], m2t[:, hs], dbf[:, hs],
                             start=True, stop=True)
                for h in range(4):
                    hs = slice(128 * h, 128 * h + 128)
                    T.op("matmul", pD[:, hs], KT[:, hs], dbf[:, hs], start=True, stop=True)
                T.inc_last(s_t)
                V.wait(s_t)
                if c == 0:
                    V.op("tensor_copy", sf32, pD[:, :])
                else:
                    V.op("tensor_tensor", sf32, sf32, pD[:, :], ALU.add)
                V.drain()
                V.op("tensor_copy", sbf, sf32)
                V.drain(inc=(s_ev, 1))
                if c == 0 and g == 0 and seq == 0:
                    dbg(18, sbf, s_ev)
                b = c % 2
                S.wait(s_t)
                S.wait(s_os[b])
                S.op("copy", osg[b], pF[:, :])
                S.drain(inc=(s_s, 1))
                mk[("doev", seq, g, c)] = s_s.n
                if c == 0 and g == 0 and seq == 0:
                    dbg(19, osg[b], s_s)
                Y.wait(s_s)
                for h in range(4):
                    Y.dma(dr["o_scr"][4 * g + h, 128 * c:128 * c + 128, :],
                          osg[b][:, 128 * h:128 * h + 128], s_os[b])
                mk[("dend", seq, g, c)] = s_t.n

        def og_phase(seq, src_res):
            ts0 = seq * N
            abf.push(); af32.push()
            g1fm = abf.alloc(N)
            gate = abf.alloc(D)
            so_t = [abf.alloc(D) for _ in range(2)]
            og1 = abf.alloc(D)
            og2 = sqs
            xcol = abf.alloc3(16, 512)
            swo = abf.alloc3(16, 512)
            sxg = [abf.alloc3(16, 128) for _ in range(2)]

            for tb in range(4):
                b = tb % 2
                Y.wait(s_t, mk.get(("gxc", seq, tb - 1), 0))
                Y.dma(xcol[:, :, :],
                      dr["xn_fm"][:, :, ts0 + 512 * tb: ts0 + 512 * tb + 512]
                      .rearrange("c p t -> p c t"), s_sb[b])
                T.wait(s_sb[b])
                T.wait(s_s, mk.get(("g1e", seq, tb - 1), 0))
                for c in range(16):
                    T.op("matmul", pA[:, :], swg1[:, c, :], xcol[:, c, :],
                         start=(c == 0), stop=(c == 15))
                T.inc_last(s_t)
                mk[("gxc", seq, tb)] = s_t.n
                S.wait(s_t)
                S.op("copy", g1fm[:, 512 * tb:512 * tb + 512], pA[:, :])
                S.drain(inc=(s_s, 1))
                mk[("g1e", seq, tb)] = s_s.n
            T.wait(s_s)

            for lt in range(16):
                b = lt % 2
                Y.wait(s_v, mk.get(("oso", seq, lt - 2), 0))
                Y.dma(so_t[b].rearrange("p (h d) -> p h d", d=128),
                      dr["o_scr"][:, 128 * lt:128 * lt + 128, :]
                      .rearrange("h t d -> t h d"), s_sb[b])
                T.wait(s_s, mk.get(("ogv", seq, lt - 1), 0))
                for ob in range(4):
                    T.op("matmul", PSF[ob][:, :], g1fm[:, 128 * lt:128 * lt + 128],
                         swg2[:, 512 * ob:512 * ob + 512], start=True, stop=True)
                T.inc_last(s_t)
                S.wait(s_t)
                for ob in range(4):
                    S.op("activation", gate[:, 512 * ob:512 * ob + 512], PSF[ob][:, :],
                         AF.Sigmoid)
                S.drain(inc=(s_s, 1))
                mk[("ogv", seq, lt)] = s_s.n
                V.wait(s_s)
                V.wait(s_sb[b])
                V.op("tensor_tensor", og1, so_t[b], gate, ALU.mult)
                V.drain()
                V.op("tensor_tensor", og2, og1, og1, ALU.mult)
                V.drain()
                V.op("tensor_reduce", colC[:, 0:16].unsqueeze(2),
                     og2.rearrange("p (h d) -> p h d", d=128), AX.X, ALU.add)
                V.drain(inc=(s_v, 1))
                mk[("oso", seq, lt)] = s_v.n
                S.wait(s_v)
                S.op("activation", colC[:, 16:32], colC[:, 0:16], AF.Ln,
                     scale=1.0 / HD_A, bias=1e-6)
                S.drain()
                S.op("activation", colC[:, 16:32], colC[:, 16:32], AF.Exp, scale=-0.5)
                S.drain(inc=(s_s, 1))
                V.wait(s_s)
                V.op("tensor_tensor", og2.rearrange("p (h d) -> p h d", d=128),
                     og1.rearrange("p (h d) -> p h d", d=128),
                     colC[:, 16:32].unsqueeze(2).broadcast_to([128, 16, 128]), ALU.mult)
                V.drain()
                V.wait(s_xn[b])
                V.op("tensor_tensor", xnt[b], og2, gbcast[:, :], ALU.mult)
                V.drain(inc=(s_v, 1))
                Y.wait(s_v)
                Y.dma(dr["og_tm"][128 * lt:128 * lt + 128, :], xnt[b], s_xn[b])
            Y.wait(s_xn[0]); Y.wait(s_xn[1])
            barrier()
            dmat_pass(src="og_tm", dst="og_fm", rows=N)
            barrier()

            for ob in range(4):
                G.wait(s_t, mk.get(("wgw", seq, ob - 1), 0))
                G.dma(swo[:, :, :],
                      dr["woaT"][:, 512 * ob:512 * ob + 512]
                      .rearrange("(c p) o -> p c o", p=128), s_gw)
                T.wait(s_gw)
                T.wait(s_v); T.wait(s_s)
                for lt in range(16):
                    b = lt % 2
                    gi = seq * 16 + lt
                    Y.wait(s_t, mk.get(("wsx", seq, ob, lt - 2), 0))
                    Y.dma(sxg[b][:, :, :], dr["og_fm"][:, :, 128 * lt:128 * lt + 128]
                          .rearrange("c p t -> p c t"), s_sa[b])
                    Y.wait(s_xt[b])
                    Y.dma(xt[b][:, 0:512],
                          src_res[128 * gi:128 * gi + 128, 512 * ob:512 * ob + 512],
                          s_xt[b])
                    T.wait(s_sa[b])
                    T.wait(s_v, mk.get(("wev", seq, ob, lt - 2), 0))
                    pp = PSF[lt % 2]
                    for c in range(16):
                        T.op("matmul", pp[:, :], sxg[b][:, c, :], swo[:, c, :],
                             start=(c == 0), stop=(c == 15))
                    T.inc_last(s_t)
                    mk[("wsx", seq, ob, lt)] = s_t.n
                    V.wait(s_t)
                    V.wait(s_xt[b])
                    V.wait(s_xw[0])
                    V.op("scalar_tensor_tensor", xnew[0][:, 0:512],
                         pp[:, :], 1.0, xt[b][:, 0:512], ALU.mult, ALU.add)
                    V.drain(inc=(s_v, 1))
                    mk[("wev", seq, ob, lt)] = s_v.n
                    Y.wait(s_v)
                    Y.dma(dr["xres_a"][128 * gi:128 * gi + 128,
                                       512 * ob:512 * ob + 512],
                          xnew[0][:, 0:512], s_xw[0])
                mk[("wgw", seq, ob)] = s_t.n
            Y.wait(s_xw[0])
            abf.pop(); af32.pop()

        def ffn_pass(src_res, dst_dram, tag):
            abf.push(); af32.push()
            sxnt = [abf.alloc3(16, 128) for _ in range(2)]
            swf = abf.alloc3(16, 512)
            sxaw = abf.alloc(128)
            xawT = abf.alloc3(TT, 128)
            slog = af32.alloc(NE)
            for i in range(TT):
                b = i % 2
                for e in (T, V, S):
                    e.wait(s_t); e.wait(s_v); e.wait(s_s)
                Y.wait(s_t, mk.get((tag, "fsx", i - 2), 0))
                Y.dma(sxnt[b][:, :, :], dr["xn_fm"][:, :, 128 * i:128 * i + 128]
                      .rearrange("c p t -> p c t"), s_sa[b])
                T.wait(s_sa[b])
                T.wait(s_s, mk.get((tag, "flg", i - 1), 0))
                T.wait(s_v, mk.get((tag, "fxa", i - 1), 0))
                for c in range(16):
                    T.op("matmul", pE[:, 0:NE], sxnt[b][:, c, :], sgw[:, c, :],
                         start=(c == 0), stop=False)
                    T.op("matmul", pF[:, 0:NE * NR], sxnt[b][:, c, :], saT[:, c, :],
                         start=(c == 0), stop=(c == 15))
                T.op("matmul", pE[:, 0:NE], c_ones[0:1, 0:128], gbrow[0:1, :],
                     start=False, stop=True)
                T.inc_last(s_t)
                mk[(tag, "fsx", i)] = s_t.n
                S.wait(s_t)
                S.op("copy", slog, pE[:, 0:NE])
                S.drain(inc=(s_s, 1))
                mk[(tag, "flg", i)] = s_s.n + 2
                V.wait(s_s)
                V.op("tensor_reduce", colC[:, 0:1], slog, AX.X, ALU.max)
                V.drain()
                V.op("tensor_scalar", colC[:, 4:4 + NE], slog, colC[:, 0:1], None, ALU.is_ge)
                V.drain()
                V.op("scalar_tensor_tensor", colC[:, 12:12 + NE], colC[:, 4:4 + NE],
                     -1e30, slog, ALU.mult, ALU.add)
                V.drain()
                V.op("tensor_reduce", colC[:, 1:2], colC[:, 12:12 + NE], AX.X, ALU.max)
                V.drain(inc=(s_v, 1))
                S.wait(s_v)
                S.op("activation", colB[:, 4:5], colC[:, 1:2], AF.Sigmoid,
                     bias=colC[:, 0:1], scale=-1.0)
                S.drain()
                S.op("activation", colB[:, 5:6], colC[:, 0:1], AF.Sigmoid,
                     bias=colC[:, 1:2], scale=-1.0)
                S.drain(inc=(s_s, 1))
                V.wait(s_s)
                V.op("tensor_scalar", colC[:, 20:20 + NE], colC[:, 12:12 + NE],
                     colC[:, 1:2], None, ALU.is_ge)
                V.drain()
                V.op("tensor_scalar", colC[:, 4:4 + NE], colC[:, 4:4 + NE],
                     colB[:, 4:5], None, ALU.mult)
                V.drain()
                V.op("scalar_tensor_tensor", colC[:, 12:12 + NE], colC[:, 20:20 + NE],
                     colB[:, 5:6], colC[:, 4:4 + NE], ALU.mult, ALU.add)
                V.drain()
                V.wait(s_t, mk.get((tag, "ftp", i - 1), 0))
                V.op("tensor_tensor", sxaw.rearrange("p (e r) -> p e r", r=NR),
                     pF[:, 0:128].rearrange("p (e r) -> p e r", r=NR),
                     colC[:, 12:12 + NE].unsqueeze(2).broadcast_to([128, NE, NR]),
                     ALU.mult)
                V.drain(inc=(s_v, 1))
                mk[(tag, "fxa", i)] = s_v.n
                if "dbg_scr" in pr_debug and tag == "f1" and i == 30:
                    dbg(8, sxaw.broadcast_to([128, 128]) if False else sxaw, s_v)
                    dbg(9, colC[:, 0:32], s_v)
                    dbg(10, slog, s_v)
                T.wait(s_v)
                T.wait(s_s, mk.get((tag, "fxt", i - 1), 0))
                T.op("transpose", pTa[:, 0:128], sxaw, c_id[:, :])
                T.inc_last(s_t)
                mk[(tag, "ftp", i)] = s_t.n
                S.wait(s_t)
                S.op("copy", xawT[:, i, :], pTa[:, 0:128])
                S.drain(inc=(s_s, 1))
                mk[(tag, "fxt", i)] = s_s.n
            T.wait(s_s); V.wait(s_s)
            if "dbg_scr" in pr_debug and tag == "f1":
                for r8 in range(8):
                    dbg(r8, xawT[:, :, :].rearrange("p n w -> p (n w)")[:, 512 * r8:512 * r8 + 512], s_s)
            for ob in range(4):
                G.wait(s_t, mk.get((tag, "fgw", ob - 1), 0))
                G.dma(swf[:, :, :],
                      dr["wfT"][:, 512 * ob:512 * ob + 512]
                      .rearrange("(c p) o -> p c o", p=128), s_gw)
                T.wait(s_gw)
                T.wait(s_v); T.wait(s_s)
                for i in range(TT):
                    b = i % 2
                    Y.wait(s_t, mk.get((tag, "f2x", ob, i - 2), 0))
                    Y.dma(sxnt[b][:, :, :], dr["xn_fm"][:, :, 128 * i:128 * i + 128]
                          .rearrange("c p t -> p c t"), s_sa[b])
                    Y.wait(s_xt[b])
                    Y.dma(xt[b][:, 0:512],
                          src_res[128 * i:128 * i + 128, 512 * ob:512 * ob + 512],
                          s_xt[b])
                    T.wait(s_sa[b])
                    T.wait(s_v, mk.get((tag, "fev", ob, i - 2), 0))
                    pp = PSF[i % 2]
                    for c in range(16):
                        T.op("matmul", pp[:, :], sxnt[b][:, c, :], swf[:, c, :],
                             start=(c == 0), stop=False)
                    T.op("matmul", pp[:, :], c_ones[0:1, 0:128],
                         fbrow[0:1, 512 * ob:512 * ob + 512], start=False, stop=False)
                    if F_LORA:
                        T.op("matmul", pp[:, :], xawT[:, i, :],
                             sball[:, 512 * ob:512 * ob + 512], start=False, stop=True)
                    else:
                        T.op("matmul", pp[:, :], c_ones[0:1, 0:128],
                             fbrow[0:1, 512 * ob:512 * ob + 512], start=False, stop=True)
                    T.inc_last(s_t)
                    mk[(tag, "f2x", ob, i)] = s_t.n
                    V.wait(s_t)
                    V.wait(s_xt[b])
                    V.wait(s_xw[0])
                    V.op("scalar_tensor_tensor", xnew[0][:, 0:512],
                         pp[:, :], 1.0, xt[b][:, 0:512], ALU.mult, ALU.add)
                    V.drain(inc=(s_v, 1))
                    mk[(tag, "fev", ob, i)] = s_v.n
                    Y.wait(s_v)
                    Y.dma(dst_dram[128 * i:128 * i + 128, 512 * ob:512 * ob + 512],
                          xnew[0][:, 0:512], s_xw[0])
                mk[(tag, "fgw", ob)] = s_t.n
            Y.wait(s_xw[0])
            abf.pop(); af32.pop()

        def attnb_seq(seq, src_res):
            ts0 = seq * N
            abf.push(); af32.push()
            kfmB = abf.alloc3(4, 2048)
            vaug = abf.alloc3(16, 8 * 65)
            qfm = abf.alloc3(4, 2048)
            expq = [abf.alloc(512) for _ in range(2)]
            stg = abf.alloc(512)
            sxnb = abf.alloc3(16, 512)
            shared = abf.alloc(16384)
            swkb = shared[:, 0:8192].rearrange("p (n w) -> p n w", w=512)
            swvb = shared[:, 8192:16384].rearrange("p (n w) -> p n w", w=512)
            swq1 = shared[:, 0:8192].rearrange("p (n w) -> p n w", w=512)
            sqfm = shared[:, 8192:16384].rearrange("p (n w) -> p n w", w=2048)
            G.dma(swkb[:, :, :], dr["wkbT"][:, :].rearrange("(c p) o -> p c o", p=128), s_gw)
            G.dma(swvb[:, :, :], dr["wvbT"][:, :].rearrange("(c p) o -> p c o", p=128), s_gw)
            V.op("memset",
                 vaug[:, :, :].rearrange("p t (h e) -> p t h e", e=65)[:, :, :, 64:65], 1.0)
            V.drain(inc=(s_v, 1))
            T.wait(s_gw)
            T.wait(s_v)

            for tb in range(4):
                b = tb % 2
                Y.wait(s_t, mk.get(("bsx", seq, tb - 1), 0))
                Y.dma(sxnb[:, :, :],
                      dr["xn_fm"][:, :, ts0 + 512 * tb: ts0 + 512 * tb + 512]
                      .rearrange("c p t -> p c t"), s_sb[b])
                T.wait(s_sb[b])
                T.wait(s_s, mk.get(("bkv", seq, tb - 1), 0))
                for ob in range(4):
                    for c in range(16):
                        T.op("matmul", PSF[ob][:, :], swkb[:, c, 128 * ob:128 * ob + 128],
                             sxnb[:, c, :], start=(c == 0), stop=(c == 15))
                for t4 in range(4):
                    T.wait(s_s, mk.get(("bvv", seq, tb, t4 - 2), 0))
                    for c in range(16):
                        T.op("matmul", PSF[4 + (t4 % 2)][:, :],
                             sxnb[:, c, 128 * t4:128 * t4 + 128],
                             swvb[:, c, :], start=(c == 0), stop=(c == 15))
                    T.inc_last(s_t)
                    S.wait(s_t)
                    S.op("copy", vaug[:, 4 * tb + t4, :]
                         .rearrange("p (h e) -> p h e", e=65)[:, :, 0:64],
                         PSF[4 + (t4 % 2)][:, :].rearrange("p (h e) -> p h e", e=64))
                    S.drain(inc=(s_s, 1))
                    mk[("bvv", seq, tb, t4)] = s_s.n
                mk[("bsx", seq, tb)] = s_t.n
                S.wait(s_t)
                for ob in range(4):
                    S.op("copy", kfmB[:, ob, 512 * tb:512 * tb + 512], PSF[ob][:, :])
                S.drain(inc=(s_s, 1))
                mk[("bkv", seq, tb)] = s_s.n
            S.drain(inc=(s_ev, 1))
            T.wait(s_ev)
            V.wait(s_ev)
            mk[("bqw", seq, -1)] = s_t.n

            for qg in range(4):
                G.wait(s_t, mk.get(("bqw", seq, qg - 1), 0))
                G.dma(swq1[:, :, :], dr["wqbT"][:, 512 * qg:512 * qg + 512]
                      .rearrange("(c p) o -> p c o", p=128), s_gw)
                T.wait(s_gw)
                for tb in range(4):
                    b = tb % 2
                    Y.wait(s_t, mk.get(("bqx", seq, qg, tb - 1), 0))
                    Y.dma(sxnb[:, :, :],
                          dr["xn_fm"][:, :, ts0 + 512 * tb: ts0 + 512 * tb + 512]
                          .rearrange("c p t -> p c t"), s_sb[b])
                    T.wait(s_sb[b])
                    T.wait(s_s, mk.get(("bqe", seq, qg, tb - 1), 0))
                    for ob in range(4):
                        for c in range(16):
                            T.op("matmul", PSF[ob][:, :],
                                 swq1[:, c, 128 * ob: 128 * ob + 128],
                                 sxnb[:, c, :], start=(c == 0), stop=(c == 15))
                    T.inc_last(s_t)
                    mk[("bqx", seq, qg, tb)] = s_t.n
                    S.wait(s_t)
                    for ob in range(4):
                        S.op("copy", qfm[:, ob, 512 * tb:512 * tb + 512], PSF[ob][:, :])
                    S.drain(inc=(s_s, 1))
                    mk[("bqe", seq, qg, tb)] = s_s.n
                mk[("bqw", seq, qg)] = s_t.n
                V.wait(s_s)
                for cb in range(4):
                    V.op("tensor_scalar", sqs, qfm[:, cb, :], 0.0, None, ALU.min)
                    V.drain(inc=(s_v, 1))
                    S.wait(s_v)
                    S.op("activation", sqfm[:, cb, :], sqs, AF.Exp)
                    S.drain(inc=(s_s, 1))
                    V.wait(s_s)
                    V.op("tensor_scalar", xnt[0], qfm[:, cb, :], 0.0, None, ALU.max)
                    V.drain()
                    V.op("scalar_tensor_tensor", sqfm[:, cb, :], sqfm[:, cb, :],
                         -1.0, xnt[0], ALU.add, ALU.add)
                    V.drain(inc=(s_v, 1))
                T.wait(s_v)
                for tb in range(4):
                    T.wait(s_s, mk.get(("bme", seq, qg, tb - 1), 0))
                    for ob in range(4):
                        for c in range(4):
                            T.op("matmul", PSF[ob][:, :], smem[:, c, 128 * ob:128 * ob + 128],
                                 sqfm[:, c, 512 * tb:512 * tb + 512],
                                 start=(c == 0), stop=(c == 3))
                    for c in range(4):
                        T.op("matmul", pE[0:1, :], smn[:, c, :],
                             sqfm[:, c, 512 * tb:512 * tb + 512],
                             start=(c == 0), stop=(c == 3))
                    T.inc_last(s_t)
                    S.wait(s_t)
                    for ob in range(4):
                        S.wait(s_os[0])
                        S.op("copy", stg, PSF[ob][:, :])
                        S.drain(inc=(s_s, 1))
                        Y.wait(s_s)
                        Y.dma(dr["amem_scr"][512 * qg + 128 * ob:512 * qg + 128 * ob + 128,
                                             512 * tb:512 * tb + 512], stg, s_os[0])
                    S.wait(s_os[0])
                    S.op("copy", stg[0:1, :], pE[0:1, :])
                    S.drain(inc=(s_s, 1))
                    Y.wait(s_s)
                    Y.dma(dr["dens_scr"][qg, 512 * tb:512 * tb + 512].unsqueeze(0),
                          stg[0:1, :], s_os[0])
                    mk[("bme", seq, qg, tb)] = s_s.n
                S.wait(s_os[0])
                S.drain(inc=(s_ev, 1))
                T.wait(s_ev)
                for h in range(NH_B):
                    ob = h // 2
                    mrow = 64 * (h % 2)
                    for sb in range(4):
                        for tb in range(16):
                            pp = tb % 2
                            T.wait(s_s, mk.get(("bex", seq, qg, h, sb, tb - 2), 0))
                            T.op("matmul", PSF[pp][:, :],
                                 kfmB[mrow:mrow + 64, ob, 128 * tb:128 * tb + 128],
                                 qfm[mrow:mrow + 64, ob, 512 * sb:512 * sb + 512],
                                 start=True, stop=True, inc=(s_t, 1))
                            S.wait(s_t)
                            S.wait(s_t, mk.get(("bav", seq, qg, h, sb, tb - 2), 0))
                            S.op("activation", expq[pp], PSF[pp][:, :], AF.Exp,
                                 scale=0.125, bias=-16.0)
                            S.drain(inc=(s_s, 1))
                            mk[("bex", seq, qg, h, sb, tb)] = s_s.n
                            T.wait(s_s)
                            T.op("matmul", pF[0:65, :],
                                 vaug[:, tb, 65 * h:65 * h + 65], expq[pp],
                                 start=(tb == 0), stop=(tb == 15))
                            T.inc_last(s_t)
                            mk[("bav", seq, qg, h, sb, tb)] = s_t.n
                        S.wait(s_t)
                        S.wait(s_os[1])
                        S.op("copy", stg[0:65, :], pF[0:65, :])
                        S.drain(inc=(s_s, 1))
                        Y.wait(s_s)
                        Y.dma(dr["adot_scr"][512 * qg + 64 * h:512 * qg + 64 * h + 64,
                                             512 * sb:512 * sb + 512],
                              stg[0:64, :], s_os[1])
                        Y.dma(dr["sums_scr"][8 * qg + h, 512 * sb:512 * sb + 512]
                              .unsqueeze(0), stg[64:65, :], s_os[1])
            abf.pop(); af32.pop()
            barrier()
            attnb_blend(seq, src_res)

        def attnb_blend(seq, src_res):
            abf.push(); af32.push()
            adot_t = [abf.alloc(D) for _ in range(2)]
            amem_t = [abf.alloc(D) for _ in range(2)]
            sumrows = abf.alloc(N)[0:32, :]
            denrows = abf.alloc(N)[0:16, :]
            t1 = abf.alloc(D)
            t2 = abf.alloc(D)
            rcp = af32.alloc(64)
            Y.dma(sumrows, dr["sums_scr"][:, :], s_sb[0])
            Y.dma(denrows[0:NQ, :], dr["dens_scr"][:, :], s_sb[0])
            T.wait(s_sb[0])
            for lt in range(16):
                b = lt % 2
                gi = seq * 16 + lt
                T.wait(s_v, mk.get(("btp", seq, lt - 1), 0))
                T.op("transpose", pTa[:, 0:32], sumrows[:, 128 * lt:128 * lt + 128],
                     c_id[0:32, 0:32])
                T.op("transpose", pTa[:, 32:48], denrows[0:16, 128 * lt:128 * lt + 128],
                     c_id[0:16, 0:16])
                T.inc_last(s_t)
                V.wait(s_t)
                V.op("tensor_copy", sqs[:, 0:32], pTa[:, 0:32])
                V.op("tensor_copy", sqs[:, 32:36], pTa[:, 32:36])
                V.drain()
                V.op("reciprocal", rcp[:, 0:32], sqs[:, 0:32])
                V.drain()
                V.op("reciprocal", rcp[:, 32:36], sqs[:, 32:36])
                V.drain(inc=(s_v, 1))
                mk[("btp", seq, lt)] = s_v.n
                Y.wait(s_v, mk.get(("bfr", seq, lt - 2), 0))
                Y.wait(s_fm[b])
                Y.op("dma_start", out=adot_t[b],
                     in_=dr["adot_scr"][:, 128 * lt:128 * lt + 128],
                     transpose=True, inc=(s_fm[b], 16))
                Y.op("dma_start", out=amem_t[b],
                     in_=dr["amem_scr"][:, 128 * lt:128 * lt + 128],
                     transpose=True, inc=(s_fm[b], 16))
                Y.wait(s_xt[b])
                Y.dma(xt[b], src_res[128 * gi:128 * gi + 128, :], s_xt[b])
                V.wait(s_fm[b])
                V.wait(s_xt[b])
                V.op("tensor_tensor", t1.rearrange("p (g e) -> p g e", e=64),
                     adot_t[b].rearrange("p (g e) -> p g e", e=64),
                     rcp[:, 0:32].unsqueeze(2).broadcast_to([128, 32, 64]), ALU.mult)
                V.drain()
                V.op("tensor_tensor", t2.rearrange("p (g e) -> p g e", e=512),
                     amem_t[b].rearrange("p (g e) -> p g e", e=512),
                     rcp[:, 32:36].unsqueeze(2).broadcast_to([128, 4, 512]), ALU.mult)
                V.drain()
                V.op("tensor_scalar", t2, t2, wcol[:, 0:1], None, ALU.mult)
                V.drain()
                V.op("scalar_tensor_tensor", sqs, t1, wcol[:, 1:2], t2,
                     ALU.mult, ALU.add)
                V.drain()
                V.wait(s_xw[0])
                V.op("tensor_tensor", xnew[0], sqs, xt[b], ALU.add)
                V.drain(inc=(s_v, 1))
                mk[("bfr", seq, lt)] = s_v.n
                Y.wait(s_v)
                Y.dma(dr["xres_b"][128 * gi:128 * gi + 128, :], xnew[0], s_xw[0])
            Y.wait(s_xw[0])
            abf.pop(); af32.pop()

        prologue(); barrier()
        if "ln1" in stages:
            ln_pass(dr["x"]); barrier()
        if "dmat1" in stages:
            dmat_pass(); barrier()
        if "attna0" in stages:
            attna_seq(0, dr["x"]); barrier()
        if "attna1" in stages:
            attna_seq(1, dr["x"]); barrier()
        if "ln2" in stages:
            ln_pass(dr["xres_a"]); barrier()
        if "dmat2" in stages:
            dmat_pass(); barrier()
        if "ffn1" in stages:
            ffn_pass(dr["xres_a"], dr["xres_b"], "f1"); barrier()
        if "ln3" in stages:
            ln_pass(dr["xres_b"]); barrier()
        if "dmat3" in stages:
            dmat_pass(); barrier()
        if "attnb0" in stages:
            attnb_seq(0, dr["xres_b"]); barrier()
        if "attnb1" in stages:
            attnb_seq(1, dr["xres_b"]); barrier()
        if "ln4" in stages:
            ln_pass(dr["xres_b"]); barrier()
        if "dmat4" in stages:
            dmat_pass(); barrier()
        if "ffn2" in stages:
            ffn_pass(dr["xres_b"], dr["out"], "f2"); barrier()
        kb.emit()
    return nc


def kernel(**inputs):
    nc = build()
    in_maps = [host_inputs(inputs, c) for c in range(NCORE)]
    res = run_bass_kernel_spmd(nc, in_maps, core_ids=list(range(NCORE)))
    return np.concatenate([r["out"].reshape(SEQ, N, D) for r in res.results], axis=0)


# version 04
# speedup vs baseline: 1.2763x; 1.2763x over previous
import numpy as np
import ml_dtypes
import concourse.bass as bass
import concourse.mybir as mybir
from concourse.bass_utils import run_bass_kernel_spmd
from contextlib import ExitStack

F32 = mybir.dt.float32
BF16 = mybir.dt.bfloat16
AF = mybir.ActivationFunctionType
ALU = mybir.AluOpType
AX = mybir.AxisListType
BF = ml_dtypes.bfloat16

NCORE = 8
SEQ = 2
N = 2048
D = 2048
R = SEQ * N
TT = R // 128
TTS = N // 128
NH_A = 16
HD_A = 128
NQ = 4
NH_B = 8
DH_B = 64
DKV = 512
NE = 8
NR = 16

ALL_STAGES = ("ln1", "dmat1", "attna0", "attna1", "ln2", "dmat2", "ffn1",
              "ln3", "dmat3", "attnb0", "attnb1", "ln4", "dmat4", "ffn2")

A_GROUPS = 4
A_HALVES = 2
A_OG = True
A_DELTA = True
A_DCHUNKS = 8
A_PROJLVL = 3
F_LORA = True


class SemC:
    __slots__ = ("h", "n", "name")
    def __init__(self, h, name):
        self.h, self.n, self.name = h, 0, name


class _Op:
    __slots__ = ("kind", "meth", "args", "kwargs", "inc")
    def __init__(self, kind, meth, args, kwargs, inc):
        self.kind, self.meth, self.args, self.kwargs, self.inc = kind, meth, args, kwargs, inc


class Eng:
    def __init__(self, kb, key):
        self.kb, self.key = kb, key

    def op(self, meth, *args, inc=None, **kwargs):
        if inc is not None:
            inc[0].n += inc[1]
        self.kb.plan[self.key].append(_Op("op", meth, args, kwargs, inc))

    def wait(self, sem, val=None):
        v = sem.n if val is None else val
        if v > 0:
            self.kb.plan[self.key].append(_Op("wait", None, (sem, v), {}, None))

    def dma(self, out, in_, inc, **kw):
        self.op("dma_start", out=out, in_=in_, inc=(inc, 16), **kw)

    def drain(self, inc=None):
        self.op("drain", inc=inc)

    def dinc(self, sem, k=1):
        self.inc_last(sem, k)

    def inc_last(self, sem, k=1):
        ops = self.kb.plan[self.key]
        assert ops and ops[-1].kind == "op" and ops[-1].inc is None, "inc_last misuse"
        sem.n += k
        ops[-1].inc = (sem, k)


class Arena:
    def __init__(self, tensor, cols):
        self.t, self.cols, self.top = tensor, cols, 0
        self.marks = []

    def alloc(self, n):
        a = self.top
        assert a + n <= self.cols, f"arena overflow: {a}+{n} > {self.cols}"
        self.top = a + n
        return self.t[:, a:a + n]

    def alloc3(self, n, w):
        return self.alloc(n * w).rearrange("p (n w) -> p n w", w=w)

    def push(self):
        self.marks.append(self.top)

    def pop(self):
        self.top = self.marks.pop()


class Kb:
    def __init__(self, nc):
        self.nc = nc
        self.plan = {k: [] for k in "TVSYG"}
        self.T = Eng(self, "T"); self.V = Eng(self, "V"); self.S = Eng(self, "S")
        self.Y = Eng(self, "Y"); self.G = Eng(self, "G")
        self.sems = []
        self.ctx = None

    def sem(self, name):
        s = SemC(self.ctx.enter_context(self.nc.semaphore(name)), name)
        self.sems.append(s)
        return s

    def sbuf(self, name, shape, dtype=F32):
        return self.ctx.enter_context(self.nc.sbuf_tensor(name, shape, dtype))

    def psum(self, name, shape, dtype=F32):
        return self.ctx.enter_context(self.nc.psum_tensor(name, shape, dtype))

    def emit(self):
        block = self.ctx.enter_context(self.nc.Block())

        def mk(key):
            def body(eng):
                for o in self.plan[key]:
                    if o.kind == "wait":
                        sem, v = o.args
                        eng.wait_ge(sem.h, v)
                    else:
                        ins = getattr(eng, o.meth)(*o.args, **o.kwargs)
                        if o.inc is not None:
                            ins.then_inc(o.inc[0].h, o.inc[1])
            return body
        block.tensor(mk("T"))
        block.vector(mk("V"))
        block.scalar(mk("S"))
        block.sync(mk("Y"))
        block.gpsimd(mk("G"))


def host_inputs(inputs, core):
    x = np.asarray(inputs["L0_attn_a_x"])
    t = lambda a: np.ascontiguousarray(np.asarray(a).T)
    return {
        "x": np.ascontiguousarray(x[2 * core: 2 * core + 2].reshape(R, D)),
        "wqaT": t(inputs["L0_attn_a_Wq"]),
        "wkaT": t(inputs["L0_attn_a_Wk"]),
        "wvaT": t(inputs["L0_attn_a_Wv"]),
        "wbetaT": t(inputs["L0_attn_a_Wbeta"]),
        "wg1T": t(inputs["L0_attn_a_Wg1"]),
        "wg2T": t(inputs["L0_attn_a_Wg2"]),
        "gnorm": np.ascontiguousarray(np.asarray(inputs["L0_attn_a_g_norm"]).reshape(1, D)),
        "woaT": t(inputs["L0_attn_a_Wo"]),
        "wqbT": t(inputs["L0_attn_b_Wq"]),
        "wkbT": t(inputs["L0_attn_b_Wk"]),
        "wvbT": t(inputs["L0_attn_b_Wv"]),
        "mem": np.ascontiguousarray(np.asarray(inputs["L0_attn_b_memory"])),
        "memnorm": np.ascontiguousarray(np.asarray(inputs["L0_attn_b_memory_norm"])),
        "memw": np.ascontiguousarray(np.asarray(inputs["L0_attn_b_memory_weight"]).reshape(1, 1)),
        "wfT": t(inputs["L0_ffn_W"]),
        "fb": np.ascontiguousarray(np.asarray(inputs["L0_ffn_b"]).reshape(1, D)),
        "aTall": np.ascontiguousarray(
            np.asarray(inputs["L0_ffn_lora_A"]).transpose(2, 0, 1).reshape(D, NE * NR)),
        "ball": np.ascontiguousarray(
            np.asarray(inputs["L0_ffn_lora_B"]).transpose(0, 2, 1).reshape(NE * NR, D)),
        "gwT": t(inputs["L0_ffn_gate_w"]),
        "gb": np.ascontiguousarray(np.asarray(inputs["L0_ffn_gate_b"]).reshape(1, NE)),
        "idb": np.eye(128, dtype=np.float32).astype(BF),
        "mask_l": np.tril(np.ones((128, 128), np.float32), -1).astype(BF),
        "mask_u": np.triu(np.ones((128, 128), np.float32), 1).astype(BF),
        "mask_ui": np.triu(np.ones((128, 128), np.float32), 0).astype(BF),
        "onesb": np.ones((1, 512), np.float32).astype(BF),
        "onesf": np.ones((1, 512), np.float32),
        "ccols": np.tile(np.array([[1e-5, 1e-6, -16.0]], np.float32), (128, 1)),
    }


IN_SHAPES = {
    "x": ((R, D), F32), "wqaT": ((D, D), F32), "wkaT": ((D, D), F32),
    "wvaT": ((D, D), F32), "wbetaT": ((D, NH_A), F32), "wg1T": ((D, 128), F32),
    "wg2T": ((128, D), F32), "gnorm": ((1, D), F32), "woaT": ((D, D), F32),
    "wqbT": ((D, D), F32), "wkbT": ((D, DKV), F32), "wvbT": ((D, DKV), F32),
    "mem": ((DKV, DKV), F32), "memnorm": ((DKV, 1), F32), "memw": ((1, 1), F32),
    "wfT": ((D, D), F32), "fb": ((1, D), F32), "aTall": ((D, NE * NR), F32),
    "ball": ((NE * NR, D), F32), "gwT": ((D, NE), F32), "gb": ((1, NE), F32),
    "idb": ((128, 128), BF16),
    "mask_l": ((128, 128), BF16), "mask_u": ((128, 128), BF16),
    "mask_ui": ((128, 128), BF16), "onesb": ((1, 512), BF16), "onesf": ((1, 512), F32),
    "ccols": ((128, 3), F32),
}

SCRATCH = {
    "xn_tm": ((R, D), BF16),
    "xn_fm": ((16, 128, R), BF16),
    "xres_a": ((R, D), F32),
    "xres_b": ((R, D), F32),
    "o_scr": ((NH_A, N, HD_A), BF16),
    "og_tm": ((N, D), BF16),
    "og_fm": ((16, 128, N), BF16),
    "adot_scr": ((D, N), BF16),
    "amem_scr": ((D, N), BF16),
    "sums_scr": ((NQ * NH_B, N), BF16),
    "dens_scr": ((NQ, N), BF16),
    "dbg_scr": ((32, 128, 512), BF16),
}


def build(debug_outs=(), stages=ALL_STAGES):
    stages = set(stages)
    nc = bass.Bass("TRN2", target_bir_lowering=False)
    kb = Kb(nc)
    with ExitStack() as ctx:
        kb.ctx = ctx
        pr_debug = set(debug_outs)
        dr = {}
        for k, (shape, dt) in IN_SHAPES.items():
            dr[k] = nc.dram_tensor(k, list(shape), dt, kind="ExternalInput")
        dr["out"] = nc.dram_tensor("out", [R, D], F32, kind="ExternalOutput")
        for k, (shape, dt) in SCRATCH.items():
            kind = "ExternalOutput" if k in debug_outs else "Internal"
            dr[k] = nc.dram_tensor(k, list(shape), dt, kind=kind)
        T, V, S, Y, G = kb.T, kb.V, kb.S, kb.Y, kb.G

        c_id = kb.sbuf("c_id", [128, 128], BF16)
        c_ml = kb.sbuf("c_ml", [128, 128], BF16)
        c_mu = kb.sbuf("c_mu", [128, 128], BF16)
        c_mui = kb.sbuf("c_mui", [128, 128], BF16)
        c_ones = kb.sbuf("c_ones", [1, 512], BF16)
        c_onesf = kb.sbuf("c_onesf", [1, 512], F32)
        c_cc = kb.sbuf("c_cc", [128, 3], F32)
        nc.const_aps.register(F32, 1e-5, c_cc[:, 0:1])
        nc.const_aps.register(F32, 1e-6, c_cc[:, 1:2])
        nc.const_aps.register(F32, -16.0, c_cc[:, 2:3])
        fbrow = kb.sbuf("fbrow", [1, D], BF16)
        gbrow = kb.sbuf("gbrow", [1, NE], BF16)
        gnrow = kb.sbuf("gnrow", [1, D], BF16)
        memwc = kb.sbuf("memwc", [1, 1], F32)
        wcol = kb.sbuf("wcol", [128, 2], F32)
        colA = kb.sbuf("colA", [128, 8], F32)
        colB = kb.sbuf("colB", [128, 20], F32)
        colC = kb.sbuf("colC", [128, 96], F32)
        gbcast = kb.sbuf("gbcast", [128, D], F32)
        swbeta = kb.sbuf("swbeta", [128, 16, NH_A], BF16)
        swg1 = kb.sbuf("swg1", [128, 16, 128], BF16)
        swg2 = kb.sbuf("swg2", [128, D], BF16)
        sgw = kb.sbuf("sgw", [128, 16, NE], BF16)
        saT = kb.sbuf("saT", [128, 16, NE * NR], BF16)
        sball = kb.sbuf("sball", [128, D], BF16)
        smem = kb.sbuf("smem", [128, 4, DKV], BF16)
        smn = kb.sbuf("smn", [128, 4, 1], BF16)

        ABF_COLS = 55500
        AF32_COLS = 14200
        abf = Arena(kb.sbuf("abf", [128, ABF_COLS], BF16), ABF_COLS)
        af32 = Arena(kb.sbuf("af32", [128, AF32_COLS], F32), AF32_COLS)
        xt = [af32.alloc(D) for _ in range(2)]
        sqs = af32.alloc(D)
        xnew = [af32.alloc(D)]
        xnt = [abf.alloc(D) for _ in range(2)]

        pA = kb.psum("pA", [128, 512], F32)
        pB = kb.psum("pB", [128, 512], F32)
        pC = kb.psum("pC", [128, 512], F32)
        pD = kb.psum("pD", [128, 512], F32)
        pE = kb.psum("pE", [128, 512], F32)
        pF = kb.psum("pF", [128, 512], F32)
        pTa = kb.psum("pTa", [128, 1024], BF16)
        pTb = kb.psum("pTb", [128, 1024], BF16)
        PSF = [pA, pB, pC, pD, pE, pF]

        s_ld = kb.sem("s_ld")
        s_gw = kb.sem("s_gw")
        s_t = kb.sem("s_t")
        s_v = kb.sem("s_v")
        s_s = kb.sem("s_s")
        s_xt = [kb.sem("s_xt0"), kb.sem("s_xt1")]
        s_xn = [kb.sem("s_xn0"), kb.sem("s_xn1")]
        s_xw = [kb.sem("s_xw0"), kb.sem("s_xw1")]
        s_fm = [kb.sem("s_fm0"), kb.sem("s_fm1")]
        s_fs = [kb.sem("s_fs0"), kb.sem("s_fs1")]
        s_sa = [kb.sem("s_sa0"), kb.sem("s_sa1")]
        s_sb = [kb.sem("s_sb0"), kb.sem("s_sb1")]
        s_os = [kb.sem("s_os0"), kb.sem("s_os1")]
        s_pe = kb.sem("s_pe")
        s_ev = kb.sem("s_ev")
        s_dbg = kb.sem("s_dbg")

        def dbg(row, ap, sem):
            if "dbg_scr" not in pr_debug:
                return
            G.wait(sem)
            wdt = ap.shape[-1] if len(ap.shape) == 2 else 512
            npp = ap.shape[0]
            G.dma(dr["dbg_scr"][row, 0:npp, 0:wdt], ap, s_dbg)

        def barrier():
            for e in (T, V, S, Y, G):
                for s in kb.sems:
                    e.wait(s)

        def prologue():
            Y.dma(c_id[:], dr["idb"][:, :], s_ld)
            Y.dma(c_ml[:], dr["mask_l"][:, :], s_ld)
            Y.dma(c_mu[:], dr["mask_u"][:, :], s_ld)
            Y.dma(c_mui[:], dr["mask_ui"][:, :], s_ld)
            Y.dma(c_ones[:], dr["onesb"][:, :], s_ld)
            Y.dma(c_onesf[:], dr["onesf"][:, :], s_ld)
            Y.dma(c_cc[:], dr["ccols"][:, :], s_ld)
            G.dma(gnrow[:], dr["gnorm"][:, :], s_gw)
            Y.dma(memwc[:], dr["memw"][:, :], s_ld)
            G.dma(fbrow[:], dr["fb"][:, :], s_gw)
            G.dma(gbrow[:], dr["gb"][:, :], s_gw)
            G.dma(swbeta[:, :, :], dr["wbetaT"][:, :].rearrange("(c p) h -> p c h", p=128), s_gw)
            G.dma(swg1[:, :, :], dr["wg1T"][:, :].rearrange("(c p) g -> p c g", p=128), s_gw)
            G.dma(swg2[:, :], dr["wg2T"][:, :], s_gw)
            G.dma(sgw[:, :, :], dr["gwT"][:, :].rearrange("(c p) e -> p c e", p=128), s_gw)
            G.dma(saT[:, :, :], dr["aTall"][:, :].rearrange("(c p) e -> p c e", p=128), s_gw)
            G.dma(sball[:, :], dr["ball"][:, :], s_gw)
            G.dma(smem[:, :, :], dr["mem"][:, :].rearrange("(c p) m -> p c m", p=128), s_gw)
            G.dma(smn[:, :, :], dr["memnorm"][:, :].rearrange("(c p) m -> p c m", p=128), s_gw,
                  allow_slow_non_contiguous=True)
            T.wait(s_ld)
            T.wait(s_gw)
            for j in range(4):
                T.op("matmul", pA[:, :], c_ones[0:1, 0:128],
                     gnrow[0:1, 512 * j:512 * j + 512], start=True, stop=True,
                     inc=(s_t, 1))
                V.wait(s_t)
                V.op("tensor_copy", gbcast[:, 512 * j:512 * j + 512], pA[:, :])
                V.drain(inc=(s_v, 1))
                T.wait(s_v)
            S.wait(s_ld)
            S.op("activation", colA[0:1, 0:1], memwc[0:1, 0:1], AF.Sigmoid)
            S.drain(inc=(s_s, 1))
            T.wait(s_s)
            T.op("matmul", pB[:, 0:1], c_onesf[0:1, 0:128], colA[0:1, 0:1],
                 start=True, stop=True, inc=(s_t, 1))
            V.wait(s_t)
            V.op("tensor_copy", wcol[:, 0:1], pB[:, 0:1])
            V.drain()
            V.op("tensor_scalar", wcol[:, 1:2], wcol[:, 0:1], -1.0, 1.0, ALU.mult, ALU.add)
            V.drain(inc=(s_v, 1))
            T.wait(s_v)

        lnm = {}

        def ln_tile(i, src_dram, b):
            S.wait(s_s, lnm.get(i - 2, 0))
            Y.dma(xt[b], src_dram[128 * i:128 * i + 128, :], s_xt[b])
            V.wait(s_xt[b])
            V.op("scalar_tensor_tensor", sqs, xt[b], 1.0, xt[b],
                 ALU.mult, ALU.mult, accum_out=colA[:, 1:2])
            V.op("tensor_reduce", colA[:, 0:1], xt[b], AX.X, ALU.add)
            V.drain()
            V.op("tensor_scalar", colA[:, 2:3], colA[:, 0:1], 1.0 / D, None, ALU.mult)
            V.drain()
            V.op("tensor_tensor", colA[:, 3:4], colA[:, 2:3], colA[:, 2:3], ALU.mult)
            V.drain()
            V.op("scalar_tensor_tensor", colA[:, 5:6], colA[:, 1:2], 1.0 / D,
                 colA[:, 3:4], ALU.mult, ALU.subtract)
            V.drain(inc=(s_v, 1))
            S.wait(s_v)
            S.op("activation", colB[:, 0:1], colA[:, 5:6], AF.Ln, bias=1e-5)
            S.drain()
            S.op("activation", colB[:, 1:2], colB[:, 0:1], AF.Exp, scale=-0.5)
            S.drain(inc=(s_s, 1))
            V.wait(s_s)
            V.op("scalar_tensor_tensor", colB[:, 2:3], colA[:, 2:3], -1.0,
                 colB[:, 1:2], ALU.mult, ALU.mult)
            V.drain(inc=(s_v, 1))
            S.wait(s_v)
            S.wait(s_xn[b])
            S.op("activation", xnt[b], xt[b], AF.Identity,
                 bias=colB[:, 2:3], scale=colB[:, 1:2])
            S.drain(inc=(s_s, 1))
            lnm[i] = s_s.n
            Y.wait(s_s)
            Y.dma(dr["xn_tm"][128 * i:128 * i + 128, :], xnt[b], s_xn[b])

        def ln_pass(src_dram):
            lnm.clear()
            for i in range(TT):
                ln_tile(i, src_dram, i % 2)
            Y.wait(s_xn[0]); Y.wait(s_xn[1])

        def dmat_pass(src="xn_tm", dst="xn_fm", rows=R):
            abf.push()
            fmb = abf.alloc(rows)
            for c in range(16):
                Y.wait(s_fs[0])
                Y.op("dma_start", out=fmb,
                     in_=dr[src][0:rows, 128 * c:128 * c + 128],
                     transpose=True, inc=(s_fm[0], 16))
                Y.wait(s_fm[0])
                Y.dma(dr[dst][c, :, 0:rows], fmb, s_fs[0])
            Y.wait(s_fs[0])
            abf.pop()

        mk = {}

        def attna_seq(seq, src_res):
            ts0 = seq * N
            abf.push(); af32.push()
            sxnt = abf.alloc3(16, 128)
            swk = abf.alloc3(16, 512)
            swv = abf.alloc3(16, 512)
            swq = abf.alloc3(16, 512)
            ktm = abf.alloc3(8, 512)
            vtm = abf.alloc3(8, 512)
            qtm = abf.alloc3(8, 512)
            sbeta = af32.alloc(16 * NH_A).rearrange("p (t h) -> p t h", h=NH_A)
            kq = abf.alloc(512)
            sqk = sqs[:, 0:512]
            uP = [af32.alloc(512) for _ in range(2)]
            nP = [af32.alloc(512) for _ in range(2)]
            u1 = [af32.alloc(512) for _ in range(2)]
            n1 = [af32.alloc(512) for _ in range(2)]
            kfm = [abf.alloc(512) for _ in range(2)]
            ktfm = [abf.alloc(512) for _ in range(2)]
            qfm = [abf.alloc(512) for _ in range(2)]
            m2t = [abf.alloc(512) for _ in range(2)]
            ktil = [abf.alloc(512) for _ in range(2)]
            xping = [af32.alloc(512) for _ in range(2)]
            dbf = abf.alloc(512)
            sf32 = af32.alloc(512)
            sbf = abf.alloc(512)
            osg = [abf.alloc(512) for _ in range(2)]

            def proj_tile(g, lt):
                sl = lt % 8
                Y.wait(s_t, mk.get(("asx", seq, g, lt - 1), 0))
                Y.dma(sxnt[:, :, :],
                      dr["xn_fm"][:, :, ts0 + 128 * lt: ts0 + 128 * lt + 128]
                      .rearrange("c p t -> p c t"), s_sa[0])
                T.wait(s_sa[0])
                T.wait(s_ev, mk.get(("akv", seq, g, lt - 1), 0))
                for c in range(16):
                    st, sp = (c == 0), (c == 15)
                    T.op("matmul", pA[:, :], sxnt[:, c, :], swk[:, c, :], start=st, stop=sp)
                    T.op("matmul", pB[:, :], sxnt[:, c, :], swv[:, c, :], start=st, stop=sp)
                    T.op("matmul", pC[:, :], sxnt[:, c, :], swq[:, c, :], start=st, stop=sp)
                    if g == 0:
                        T.op("matmul", pD[:, 0:NH_A], sxnt[:, c, :],
                             swbeta[:, c, :], start=st, stop=sp)
                T.inc_last(s_t)
                mk[("asx", seq, g, lt)] = s_t.n
                S.wait(s_t)
                if A_PROJLVL >= 2:
                    S.op("activation", kq, pA[:, :], AF.Silu)
                    S.op("copy", vtm[:, sl, :], pB[:, :])
                    S.op("activation", qtm[:, sl, :], pC[:, :], AF.Silu)
                    if g == 0:
                        S.op("activation", colC[:, 16:16 + NH_A], pD[:, 0:NH_A], AF.Sigmoid)
                S.drain(inc=(s_s, 1))
                if A_PROJLVL < 3:
                    V.wait(s_s)
                    V.drain(inc=(s_ev, 1))
                    mk[("akv", seq, g, lt)] = s_ev.n
                    return
                if g == 0:
                    V.wait(s_s)
                    V.op("tensor_scalar", sbeta[:, lt, :], colC[:, 16:16 + NH_A],
                         2.0, None, ALU.mult)
                    V.drain()
                V.wait(s_s)
                V.op("tensor_tensor", sqk, kq, kq, ALU.mult)
                V.drain()
                V.op("tensor_reduce", colC[:, 0:4].unsqueeze(2),
                     sqk.rearrange("p (h d) -> p h d", d=128), AX.X, ALU.add)
                V.drain(inc=(s_v, 1))
                S.wait(s_v)
                S.op("activation", colC[:, 4:8], colC[:, 0:4], AF.Ln, bias=1e-6)
                S.drain()
                S.op("activation", colC[:, 8:12], colC[:, 4:8], AF.Exp, scale=-0.5)
                S.drain(inc=(s_s, 1))
                V.wait(s_s)
                V.op("tensor_tensor", ktm[:, sl, :].rearrange("p (h d) -> p h d", d=128),
                     kq.rearrange("p (h d) -> p h d", d=128),
                     colC[:, 8:12].unsqueeze(2).broadcast_to([128, 4, 128]),
                     ALU.mult)
                V.drain(inc=(s_ev, 1))
                mk[("akv", seq, g, lt)] = s_ev.n

            for g in range(A_GROUPS):
                G.wait(s_t)
                G.dma(swk[:, :, :], dr["wkaT"][:, 512 * g:512 * g + 512]
                      .rearrange("(c p) o -> p c o", p=128), s_gw)
                G.dma(swv[:, :, :], dr["wvaT"][:, 512 * g:512 * g + 512]
                      .rearrange("(c p) o -> p c o", p=128), s_gw)
                G.dma(swq[:, :, :], dr["wqaT"][:, 512 * g:512 * g + 512]
                      .rearrange("(c p) o -> p c o", p=128), s_gw)
                T.wait(s_gw)
                for half in range(A_HALVES):
                    T.wait(s_v); T.wait(s_s)
                    for lt in range(8 * half, 8 * half + 8):
                        proj_tile(g, lt)
                    T.wait(s_ev); V.wait(s_ev); S.wait(s_ev)
                    if A_DELTA:
                        delta_group(seq, g, half, ktm, vtm, qtm, sbeta, uP, nP, u1, n1,
                                    kfm, ktfm, qfm, m2t, ktil, xping, dbf, sf32, sbf, osg)
                barrier()
            abf.pop(); af32.pop()
            if A_OG:
                og_phase(seq, src_res)

        def delta_group(seq, g, half, ktm, vtm, qtm, sbeta, uP, nP, u1, n1,
                        kfm, ktfm, qfm, m2t, ktil, xping, dbf, sf32, sbf, osg):
            q3 = lambda ap: ap.rearrange("p (h d) -> p h d", d=128)
            mu_b = c_mu[:, :].unsqueeze(1).broadcast_to([128, 4, 128])
            ml_b = c_ml[:, :].unsqueeze(1).broadcast_to([128, 4, 128])
            mui_b = c_mui[:, :].unsqueeze(1).broadcast_to([128, 4, 128])
            for c in range(8 * half, 8 * half + A_DCHUNKS):
                sl = c % 8
                p = c % 2
                KT = ktm[:, sl, :]
                VT = vtm[:, sl, :]
                QT = qtm[:, sl, :]
                bsl = sbeta[:, c, 4 * g:4 * g + 4].unsqueeze(2).broadcast_to([128, 4, 128])
                V.wait(s_t, mk.get(("dend", seq, g, c - 1), 0))
                V.op("tensor_tensor", q3(ktil[p]), q3(KT), bsl, ALU.mult)
                V.dinc(s_v)
                T.wait(s_v)
                T.wait(s_s, mk.get(("dtp", seq, g, c - 1), 0))
                for h in range(4):
                    hs = slice(128 * h, 128 * h + 128)
                    T.op("transpose", pTa[:, hs], KT[:, hs], c_id[:, :])
                    T.op("transpose", pTa[:, 512 + 128 * h:640 + 128 * h],
                         ktil[p][:, hs], c_id[:, :])
                    T.op("transpose", pTb[:, hs], QT[:, hs], c_id[:, :])
                T.inc_last(s_t)
                S.wait(s_t)
                S.wait(s_t, mk.get(("dend", seq, g, c - 1), 0))
                S.op("copy", kfm[p], pTa[:, 0:512])
                S.op("copy", ktfm[p], pTa[:, 512:1024])
                S.op("copy", qfm[p], pTb[:, 0:512])
                S.dinc(s_s)
                mk[("dtp", seq, g, c)] = s_s.n
                T.wait(s_s)
                T.wait(s_v, mk.get(("dmsk", seq, g, c - 1), 0))
                for h in range(4):
                    hs = slice(128 * h, 128 * h + 128)
                    T.op("matmul", pA[:, hs], kfm[p][:, hs], ktfm[p][:, hs], start=True, stop=True)
                    T.op("matmul", pB[:, hs], ktfm[p][:, hs], kfm[p][:, hs], start=True, stop=True)
                    T.op("matmul", pC[:, hs], kfm[p][:, hs], qfm[p][:, hs], start=True, stop=True)
                T.inc_last(s_t)
                V.wait(s_t)
                V.op("tensor_tensor", q3(u1[p]), q3(pA[:, :]), mu_b, ALU.mult)
                V.op("tensor_tensor", q3(n1[p]), q3(pB[:, :]), ml_b, ALU.mult)
                V.op("tensor_tensor", q3(m2t[p]), q3(pC[:, :]), mui_b, ALU.mult)
                V.dinc(s_v)
                mk[("dmsk", seq, g, c)] = s_v.n

                if c == 0:
                    V.op("tensor_tensor", q3(xping[0]), q3(VT), bsl, ALU.mult)
                    V.drain(inc=(s_v, 1))
                else:
                    T.wait(s_ev)
                    T.wait(s_s, mk[("doev", seq, g, c - 1)] if
                           ("doev", seq, g, c - 1) in mk else 0)
                    for h in range(4):
                        hs = slice(128 * h, 128 * h + 128)
                        T.op("matmul", pF[:, hs], kfm[p][:, hs], sbf[:, hs],
                             start=True, stop=True)
                    T.inc_last(s_t)
                    V.wait(s_t)
                    V.op("tensor_tensor", sqs[:, 0:512], VT, pF[:, :], ALU.subtract)
                    V.drain()
                    V.op("tensor_tensor", q3(xping[0]),
                         sqs[:, 0:512].rearrange("p (h d) -> p h d", d=128),
                         bsl, ALU.mult)
                    V.dinc(s_v)

                xcur = 0
                cur = 0
                first = True
                stages = [(2, 1), (4, 2), (8, 4), (16, 8), (32, 16), (64, 32),
                          (None, 64)]
                for sq_lv, ap_lv in stages:
                    T.wait(s_v)
                    T.wait(s_s)
                    usrc = (u1, n1, c % 2) if ap_lv == 1 else (uP, nP, cur)
                    for h in range(4):
                        hs = slice(128 * h, 128 * h + 128)
                        if sq_lv is not None:
                            T.op("matmul", pD[:, hs], usrc[1][usrc[2]][:, hs],
                                 usrc[0][usrc[2]][:, hs], start=True, stop=True)
                            if sq_lv < 64:
                                T.op("matmul", pE[:, hs], usrc[0][usrc[2]][:, hs],
                                     usrc[1][usrc[2]][:, hs], start=True, stop=True)
                        T.op("matmul", pF[:, hs], usrc[0][usrc[2]][:, hs],
                             xping[xcur][:, hs], start=True, stop=True)
                    T.inc_last(s_t)
                    V.wait(s_t)
                    V.op("tensor_tensor", xping[1 - xcur], xping[xcur], pF[:, :],
                         ALU.subtract if ap_lv == 1 else ALU.add)
                    V.dinc(s_v)
                    xcur = 1 - xcur
                    if sq_lv is not None:
                        S.wait(s_t)
                        nxt = 0 if ap_lv == 1 else 1 - cur
                        S.op("copy", uP[nxt], pD[:, :])
                        if sq_lv < 64:
                            S.op("copy", nP[nxt], pE[:, :])
                        S.dinc(s_s)
                        cur = nxt
                delta = xping[xcur]
                V.op("tensor_copy", dbf, delta)
                V.dinc(s_v)
                T.wait(s_v)
                T.wait(s_s, mk[("doev", seq, g, c - 1)] if
                       ("doev", seq, g, c - 1) in mk else 0)
                for h in range(4):
                    hs = slice(128 * h, 128 * h + 128)
                    if c > 0:
                        T.op("matmul", pF[:, hs], qfm[p][:, hs], sbf[:, hs],
                             start=True, stop=False)
                        T.op("matmul", pF[:, hs], m2t[p][:, hs], dbf[:, hs],
                             start=False, stop=True)
                    else:
                        T.op("matmul", pF[:, hs], m2t[p][:, hs], dbf[:, hs],
                             start=True, stop=True)
                for h in range(4):
                    hs = slice(128 * h, 128 * h + 128)
                    T.op("matmul", pD[:, hs], KT[:, hs], dbf[:, hs], start=True, stop=True)
                T.inc_last(s_t)
                V.wait(s_t)
                if c == 0:
                    V.op("tensor_copy", sf32, pD[:, :])
                else:
                    V.op("tensor_tensor", sf32, sf32, pD[:, :], ALU.add)
                V.drain()
                V.op("tensor_copy", sbf, sf32)
                V.dinc(s_ev)

                b = c % 2
                S.wait(s_t)
                S.wait(s_os[b])
                S.op("copy", osg[b], pF[:, :])
                S.dinc(s_s)
                mk[("doev", seq, g, c)] = s_s.n
                Y.wait(s_s)
                for h in range(4):
                    Y.dma(dr["o_scr"][4 * g + h, 128 * c:128 * c + 128, :],
                          osg[b][:, 128 * h:128 * h + 128], s_os[b])
                mk[("dend", seq, g, c)] = s_t.n

        def og_phase(seq, src_res):
            ts0 = seq * N
            abf.push(); af32.push()
            g1fm = abf.alloc(N)
            gate = abf.alloc(D)
            so_t = [abf.alloc(D) for _ in range(2)]
            og1 = abf.alloc(D)
            og2 = sqs
            xcol = abf.alloc3(16, 512)
            swo = abf.alloc3(16, 512)
            sxg = [abf.alloc3(16, 128) for _ in range(2)]

            for tb in range(4):
                b = tb % 2
                Y.wait(s_t, mk.get(("gxc", seq, tb - 1), 0))
                Y.dma(xcol[:, :, :],
                      dr["xn_fm"][:, :, ts0 + 512 * tb: ts0 + 512 * tb + 512]
                      .rearrange("c p t -> p c t"), s_sb[b])
                T.wait(s_sb[b])
                T.wait(s_s, mk.get(("g1e", seq, tb - 1), 0))
                for c in range(16):
                    T.op("matmul", pA[:, :], swg1[:, c, :], xcol[:, c, :],
                         start=(c == 0), stop=(c == 15))
                T.inc_last(s_t)
                mk[("gxc", seq, tb)] = s_t.n
                S.wait(s_t)
                S.op("copy", g1fm[:, 512 * tb:512 * tb + 512], pA[:, :])
                S.drain(inc=(s_s, 1))
                mk[("g1e", seq, tb)] = s_s.n
            T.wait(s_s)

            for lt in range(16):
                b = lt % 2
                Y.wait(s_v, mk.get(("oso", seq, lt - 2), 0))
                Y.dma(so_t[b].rearrange("p (h d) -> p h d", d=128),
                      dr["o_scr"][:, 128 * lt:128 * lt + 128, :]
                      .rearrange("h t d -> t h d"), s_sb[b])
                T.wait(s_s, mk.get(("ogv", seq, lt - 1), 0))
                for ob in range(4):
                    T.op("matmul", PSF[ob][:, :], g1fm[:, 128 * lt:128 * lt + 128],
                         swg2[:, 512 * ob:512 * ob + 512], start=True, stop=True)
                T.inc_last(s_t)
                S.wait(s_t)
                for ob in range(4):
                    S.op("activation", gate[:, 512 * ob:512 * ob + 512], PSF[ob][:, :],
                         AF.Sigmoid)
                S.drain(inc=(s_s, 1))
                mk[("ogv", seq, lt)] = s_s.n
                V.wait(s_s)
                V.wait(s_sb[b])
                V.op("tensor_tensor", og1, so_t[b], gate, ALU.mult)
                V.drain()
                V.op("tensor_tensor", og2, og1, og1, ALU.mult)
                V.drain()
                V.op("tensor_reduce", colC[:, 0:16].unsqueeze(2),
                     og2.rearrange("p (h d) -> p h d", d=128), AX.X, ALU.add)
                V.drain(inc=(s_v, 1))
                mk[("oso", seq, lt)] = s_v.n
                S.wait(s_v)
                S.op("activation", colC[:, 16:32], colC[:, 0:16], AF.Ln,
                     scale=1.0 / HD_A, bias=1e-6)
                S.drain()
                S.op("activation", colC[:, 16:32], colC[:, 16:32], AF.Exp, scale=-0.5)
                S.drain(inc=(s_s, 1))
                V.wait(s_s)
                V.op("tensor_tensor", og2.rearrange("p (h d) -> p h d", d=128),
                     og1.rearrange("p (h d) -> p h d", d=128),
                     colC[:, 16:32].unsqueeze(2).broadcast_to([128, 16, 128]), ALU.mult)
                V.drain()
                V.wait(s_xn[b])
                V.op("tensor_tensor", xnt[b], og2, gbcast[:, :], ALU.mult)
                V.drain(inc=(s_v, 1))
                Y.wait(s_v)
                Y.dma(dr["og_tm"][128 * lt:128 * lt + 128, :], xnt[b], s_xn[b])
            Y.wait(s_xn[0]); Y.wait(s_xn[1])
            barrier()
            dmat_pass(src="og_tm", dst="og_fm", rows=N)
            barrier()

            for ob in range(4):
                G.wait(s_t, mk.get(("wgw", seq, ob - 1), 0))
                G.dma(swo[:, :, :],
                      dr["woaT"][:, 512 * ob:512 * ob + 512]
                      .rearrange("(c p) o -> p c o", p=128), s_gw)
                T.wait(s_gw)
                T.wait(s_v); T.wait(s_s)
                for lt in range(16):
                    b = lt % 2
                    gi = seq * 16 + lt
                    Y.wait(s_t, mk.get(("wsx", seq, ob, lt - 2), 0))
                    Y.dma(sxg[b][:, :, :], dr["og_fm"][:, :, 128 * lt:128 * lt + 128]
                          .rearrange("c p t -> p c t"), s_sa[b])
                    Y.wait(s_xt[b])
                    Y.dma(xt[b][:, 0:512],
                          src_res[128 * gi:128 * gi + 128, 512 * ob:512 * ob + 512],
                          s_xt[b])
                    T.wait(s_sa[b])
                    T.wait(s_v, mk.get(("wev", seq, ob, lt - 2), 0))
                    pp = PSF[lt % 2]
                    for c in range(16):
                        T.op("matmul", pp[:, :], sxg[b][:, c, :], swo[:, c, :],
                             start=(c == 0), stop=(c == 15))
                    T.inc_last(s_t)
                    mk[("wsx", seq, ob, lt)] = s_t.n
                    V.wait(s_t)
                    V.wait(s_xt[b])
                    V.wait(s_xw[0])
                    V.op("scalar_tensor_tensor", xnew[0][:, 0:512],
                         pp[:, :], 1.0, xt[b][:, 0:512], ALU.mult, ALU.add)
                    V.drain(inc=(s_v, 1))
                    mk[("wev", seq, ob, lt)] = s_v.n
                    Y.wait(s_v)
                    Y.dma(dr["xres_a"][128 * gi:128 * gi + 128,
                                       512 * ob:512 * ob + 512],
                          xnew[0][:, 0:512], s_xw[0])
                mk[("wgw", seq, ob)] = s_t.n
            Y.wait(s_xw[0])
            abf.pop(); af32.pop()

        def ffn_pass(src_res, dst_dram, tag):
            abf.push(); af32.push()
            sxnt = [abf.alloc3(16, 128) for _ in range(4)]
            swf = abf.alloc3(16, 512)
            sxaw = abf.alloc(512)
            xawT = abf.alloc3(TT, 128)
            slog = af32.alloc(32)
            for i4 in range(0, TT, 4):
                Y.wait(s_t, mk.get((tag, "fsx", i4 - 4), 0))
                for j in range(4):
                    i = i4 + j
                    Y.dma(sxnt[j][:, :, :], dr["xn_fm"][:, :, 128 * i:128 * i + 128]
                          .rearrange("c p t -> p c t"), s_sa[j % 2])
                T.wait(s_sa[0]); T.wait(s_sa[1])
                T.wait(s_s, mk.get((tag, "flg", i4 - 4), 0))
                T.wait(s_v, mk.get((tag, "fxa", i4 - 4), 0))
                for j in range(4):
                    for c in range(16):
                        T.op("matmul", pE[:, 8 * j:8 * j + 8], sxnt[j][:, c, :],
                             sgw[:, c, :], start=(c == 0), stop=False)
                    T.op("matmul", pE[:, 8 * j:8 * j + 8], c_ones[0:1, 0:128],
                         gbrow[0:1, :], start=False, stop=True)
                for j in range(4):
                    for c in range(16):
                        T.op("matmul", pF[:, 128 * j:128 * j + 128], sxnt[j][:, c, :],
                             saT[:, c, :], start=(c == 0), stop=(c == 15))
                T.inc_last(s_t)
                mk[(tag, "fsx", i4)] = s_t.n
                S.wait(s_t)
                S.op("copy", slog, pE[:, 0:32])
                S.dinc(s_s)
                mk[(tag, "flg", i4)] = s_s.n
                l3 = slog.rearrange("p (t e) -> p t e", e=NE)
                V.wait(s_s)
                V.op("tensor_reduce", colB[:, 0:4].unsqueeze(2), l3, AX.X, ALU.max)
                V.drain()
                V.op("tensor_tensor", colC[:, 0:32].rearrange("p (t e) -> p t e", e=NE),
                     l3, colB[:, 0:4].unsqueeze(2).broadcast_to([128, 4, NE]), ALU.is_ge)
                V.drain()
                V.op("scalar_tensor_tensor", colC[:, 32:64], colC[:, 0:32],
                     -1e30, slog, ALU.mult, ALU.add)
                V.drain()
                V.op("tensor_reduce", colB[:, 4:8].unsqueeze(2),
                     colC[:, 32:64].rearrange("p (t e) -> p t e", e=NE), AX.X, ALU.max)
                V.drain()
                V.op("tensor_tensor", colC[:, 64:96].rearrange("p (t e) -> p t e", e=NE),
                     colC[:, 32:64].rearrange("p (t e) -> p t e", e=NE),
                     colB[:, 4:8].unsqueeze(2).broadcast_to([128, 4, NE]), ALU.is_ge)
                V.drain()
                V.op("tensor_tensor", colB[:, 8:12], colB[:, 0:4], colB[:, 4:8],
                     ALU.subtract)
                V.drain(inc=(s_v, 1))
                S.wait(s_v)
                S.op("activation", colB[:, 12:16], colB[:, 8:12], AF.Sigmoid)
                S.drain()
                S.op("activation", colB[:, 16:20], colB[:, 8:12], AF.Sigmoid, scale=-1.0)
                S.drain(inc=(s_s, 1))
                V.wait(s_s)
                V.op("tensor_tensor", colC[:, 0:32].rearrange("p (t e) -> p t e", e=NE),
                     colC[:, 0:32].rearrange("p (t e) -> p t e", e=NE),
                     colB[:, 12:16].unsqueeze(2).broadcast_to([128, 4, NE]), ALU.mult)
                V.drain()
                V.op("tensor_tensor", colC[:, 64:96].rearrange("p (t e) -> p t e", e=NE),
                     colC[:, 64:96].rearrange("p (t e) -> p t e", e=NE),
                     colB[:, 16:20].unsqueeze(2).broadcast_to([128, 4, NE]), ALU.mult)
                V.drain()
                V.op("tensor_tensor", colC[:, 0:32], colC[:, 0:32], colC[:, 64:96],
                     ALU.add)
                V.drain()
                V.wait(s_t, mk.get((tag, "ftp", i4 - 4), 0))
                V.op("tensor_tensor", sxaw.rearrange("p (t e r) -> p (t e) r", t=4, e=NE, r=NR),
                     pF[:, 0:512].rearrange("p (t e r) -> p (t e) r", t=4, e=NE, r=NR),
                     colC[:, 0:32].unsqueeze(2).broadcast_to([128, 32, NR]), ALU.mult)
                V.dinc(s_v)
                mk[(tag, "fxa", i4)] = s_v.n
                T.wait(s_v)
                T.wait(s_s, mk.get((tag, "fxt", i4 - 4), 0))
                for j in range(4):
                    T.op("transpose", pTa[:, 128 * j:128 * j + 128],
                         sxaw[:, 128 * j:128 * j + 128], c_id[:, :])
                T.inc_last(s_t)
                mk[(tag, "ftp", i4)] = s_t.n
                S.wait(s_t)
                S.op("copy", xawT[:, i4:i4 + 4, :].rearrange("p n w -> p (n w)"),
                     pTa[:, 0:512])
                S.dinc(s_s)
                mk[(tag, "fxt", i4)] = s_s.n
            T.wait(s_s); V.wait(s_s)
            if "dbg_scr" in pr_debug and tag == "f1":
                for r8 in range(8):
                    dbg(r8, xawT[:, :, :].rearrange("p n w -> p (n w)")[:, 512 * r8:512 * r8 + 512], s_s)
            for ob in range(4):
                G.wait(s_t, mk.get((tag, "fgw", ob - 1), 0))
                G.dma(swf[:, :, :],
                      dr["wfT"][:, 512 * ob:512 * ob + 512]
                      .rearrange("(c p) o -> p c o", p=128), s_gw)
                T.wait(s_gw)
                T.wait(s_v); T.wait(s_s)
                for i in range(TT):
                    b = i % 2
                    Y.wait(s_t, mk.get((tag, "f2x", ob, i - 2),
                                        mk[(tag, "fsx", TT - 4)]))
                    Y.dma(sxnt[b][:, :, :], dr["xn_fm"][:, :, 128 * i:128 * i + 128]
                          .rearrange("c p t -> p c t"), s_sa[b])
                    Y.wait(s_xt[b])
                    Y.dma(xt[b][:, 0:512],
                          src_res[128 * i:128 * i + 128, 512 * ob:512 * ob + 512],
                          s_xt[b])
                    T.wait(s_sa[b])
                    T.wait(s_v, mk.get((tag, "fev", ob, i - 2), 0))
                    pp = PSF[i % 2]
                    for c in range(16):
                        T.op("matmul", pp[:, :], sxnt[b][:, c, :], swf[:, c, :],
                             start=(c == 0), stop=False)
                    T.op("matmul", pp[:, :], c_ones[0:1, 0:128],
                         fbrow[0:1, 512 * ob:512 * ob + 512], start=False, stop=False)
                    if F_LORA:
                        T.op("matmul", pp[:, :], xawT[:, i, :],
                             sball[:, 512 * ob:512 * ob + 512], start=False, stop=True)
                    else:
                        T.op("matmul", pp[:, :], c_ones[0:1, 0:128],
                             fbrow[0:1, 512 * ob:512 * ob + 512], start=False, stop=True)
                    T.inc_last(s_t)
                    mk[(tag, "f2x", ob, i)] = s_t.n
                    V.wait(s_t)
                    V.wait(s_xt[b])
                    V.wait(s_xw[0])
                    V.op("scalar_tensor_tensor", xnew[0][:, 0:512],
                         pp[:, :], 1.0, xt[b][:, 0:512], ALU.mult, ALU.add)
                    V.drain(inc=(s_v, 1))
                    mk[(tag, "fev", ob, i)] = s_v.n
                    Y.wait(s_v)
                    Y.dma(dst_dram[128 * i:128 * i + 128, 512 * ob:512 * ob + 512],
                          xnew[0][:, 0:512], s_xw[0])
                mk[(tag, "fgw", ob)] = s_t.n
            Y.wait(s_xw[0])
            abf.pop(); af32.pop()

        def attnb_seq(seq, src_res):
            ts0 = seq * N
            abf.push(); af32.push()
            kfmB = abf.alloc3(4, 2048)
            vaug = abf.alloc3(16, 8 * 65)
            qfm = abf.alloc3(4, 2048)
            expq = [abf.alloc(512) for _ in range(2)]
            stg = [abf.alloc(512) for _ in range(2)]
            sxnb = abf.alloc3(16, 512)
            shared = abf.alloc(16384)
            swkb = shared[:, 0:8192].rearrange("p (n w) -> p n w", w=512)
            swvb = shared[:, 8192:16384].rearrange("p (n w) -> p n w", w=512)
            swq1 = shared[:, 0:8192].rearrange("p (n w) -> p n w", w=512)
            sqfm = shared[:, 8192:16384].rearrange("p (n w) -> p n w", w=2048)
            G.dma(swkb[:, :, :], dr["wkbT"][:, :].rearrange("(c p) o -> p c o", p=128), s_gw)
            G.dma(swvb[:, :, :], dr["wvbT"][:, :].rearrange("(c p) o -> p c o", p=128), s_gw)
            V.op("memset",
                 vaug[:, :, :].rearrange("p t (h e) -> p t h e", e=65)[:, :, :, 64:65], 1.0)
            V.drain(inc=(s_v, 1))
            T.wait(s_gw)
            T.wait(s_v)

            for tb in range(4):
                b = tb % 2
                Y.wait(s_t, mk.get(("bsx", seq, tb - 1), 0))
                Y.dma(sxnb[:, :, :],
                      dr["xn_fm"][:, :, ts0 + 512 * tb: ts0 + 512 * tb + 512]
                      .rearrange("c p t -> p c t"), s_sb[b])
                T.wait(s_sb[b])
                T.wait(s_s, mk.get(("bkv", seq, tb - 1), 0))
                for ob in range(4):
                    for c in range(16):
                        T.op("matmul", PSF[ob][:, :], swkb[:, c, 128 * ob:128 * ob + 128],
                             sxnb[:, c, :], start=(c == 0), stop=(c == 15))
                for t4 in range(4):
                    T.wait(s_s, mk.get(("bvv", seq, tb, t4 - 2), 0))
                    for c in range(16):
                        T.op("matmul", PSF[4 + (t4 % 2)][:, :],
                             sxnb[:, c, 128 * t4:128 * t4 + 128],
                             swvb[:, c, :], start=(c == 0), stop=(c == 15))
                    T.inc_last(s_t)
                    S.wait(s_t)
                    S.op("copy", vaug[:, 4 * tb + t4, :]
                         .rearrange("p (h e) -> p h e", e=65)[:, :, 0:64],
                         PSF[4 + (t4 % 2)][:, :].rearrange("p (h e) -> p h e", e=64))
                    S.drain(inc=(s_s, 1))
                    mk[("bvv", seq, tb, t4)] = s_s.n
                mk[("bsx", seq, tb)] = s_t.n
                S.wait(s_t)
                for ob in range(4):
                    S.op("copy", kfmB[:, ob, 512 * tb:512 * tb + 512], PSF[ob][:, :])
                S.drain(inc=(s_s, 1))
                mk[("bkv", seq, tb)] = s_s.n
            S.drain(inc=(s_ev, 1))
            T.wait(s_ev)
            V.wait(s_ev)
            mk[("bqw", seq, -1)] = s_t.n

            for qg in range(4):
                G.wait(s_t, mk.get(("bqw", seq, qg - 1), 0))
                G.dma(swq1[:, :, :], dr["wqbT"][:, 512 * qg:512 * qg + 512]
                      .rearrange("(c p) o -> p c o", p=128), s_gw)
                T.wait(s_gw)
                for tb in range(4):
                    b = tb % 2
                    Y.wait(s_t, mk.get(("bqx", seq, qg, tb - 1), 0))
                    Y.dma(sxnb[:, :, :],
                          dr["xn_fm"][:, :, ts0 + 512 * tb: ts0 + 512 * tb + 512]
                          .rearrange("c p t -> p c t"), s_sb[b])
                    T.wait(s_sb[b])
                    T.wait(s_s, mk.get(("bqe", seq, qg, tb - 1), 0))
                    for ob in range(4):
                        for c in range(16):
                            T.op("matmul", PSF[ob][:, :],
                                 swq1[:, c, 128 * ob: 128 * ob + 128],
                                 sxnb[:, c, :], start=(c == 0), stop=(c == 15))
                    T.inc_last(s_t)
                    mk[("bqx", seq, qg, tb)] = s_t.n
                    S.wait(s_t)
                    for ob in range(4):
                        S.op("copy", qfm[:, ob, 512 * tb:512 * tb + 512], PSF[ob][:, :])
                    S.drain(inc=(s_s, 1))
                    mk[("bqe", seq, qg, tb)] = s_s.n
                mk[("bqw", seq, qg)] = s_t.n
                V.wait(s_s)
                for cb in range(4):
                    V.op("tensor_scalar", sqs, qfm[:, cb, :], 0.0, None, ALU.min)
                    V.drain(inc=(s_v, 1))
                    S.wait(s_v)
                    S.op("activation", sqfm[:, cb, :], sqs, AF.Exp)
                    S.drain(inc=(s_s, 1))
                    V.wait(s_s)
                    V.op("tensor_scalar", xnt[0], qfm[:, cb, :], 0.0, None, ALU.max)
                    V.drain()
                    V.op("scalar_tensor_tensor", sqfm[:, cb, :], sqfm[:, cb, :],
                         -1.0, xnt[0], ALU.add, ALU.add)
                    V.drain(inc=(s_v, 1))
                T.wait(s_v)
                for tb in range(4):
                    T.wait(s_s, mk.get(("bme", seq, qg, tb - 1), 0))
                    for ob in range(4):
                        for c in range(4):
                            T.op("matmul", PSF[ob][:, :], smem[:, c, 128 * ob:128 * ob + 128],
                                 sqfm[:, c, 512 * tb:512 * tb + 512],
                                 start=(c == 0), stop=(c == 3))
                    for c in range(4):
                        T.op("matmul", pE[0:1, :], smn[:, c, :],
                             sqfm[:, c, 512 * tb:512 * tb + 512],
                             start=(c == 0), stop=(c == 3))
                    T.inc_last(s_t)
                    S.wait(s_t)
                    for ob in range(4):
                        S.wait(s_os[ob % 2])
                        S.op("copy", stg[ob % 2], PSF[ob][:, :])
                        S.dinc(s_s)
                        Y.wait(s_s)
                        Y.dma(dr["amem_scr"][512 * qg + 128 * ob:512 * qg + 128 * ob + 128,
                                             512 * tb:512 * tb + 512], stg[ob % 2], s_os[ob % 2])
                    S.wait(s_os[0])
                    S.op("copy", stg[0][0:1, :], pE[0:1, :])
                    S.dinc(s_s)
                    Y.wait(s_s)
                    Y.dma(dr["dens_scr"][qg, 512 * tb:512 * tb + 512].unsqueeze(0),
                          stg[0][0:1, :], s_os[0])
                    mk[("bme", seq, qg, tb)] = s_s.n
                S.wait(s_os[0])
                S.drain(inc=(s_ev, 1))
                T.wait(s_ev)
                for h in range(NH_B):
                    ob = h // 2
                    mrow = 64 * (h % 2)
                    for sb in range(4):
                        def sc_mm(tb):
                            T.wait(s_s, mk.get(("bex", seq, qg, h, sb, tb - 2), 0))
                            T.op("matmul", PSF[tb % 2][:, :],
                                 kfmB[mrow:mrow + 64, ob, 128 * tb:128 * tb + 128],
                                 qfm[mrow:mrow + 64, ob, 512 * sb:512 * sb + 512],
                                 start=True, stop=True, inc=(s_t, 1))
                            mk[("bsc", seq, qg, h, sb, tb)] = s_t.n
                        def exp_op(tb):
                            S.wait(s_t, mk[("bsc", seq, qg, h, sb, tb)])
                            S.wait(s_t, mk.get(("bav", seq, qg, h, sb, tb - 2), 0))
                            S.op("activation", expq[tb % 2], PSF[tb % 2][:, :], AF.Exp,
                                 scale=0.125, bias=-16.0)
                            S.dinc(s_s)
                            mk[("bex", seq, qg, h, sb, tb)] = s_s.n
                        def av_mm(tb):
                            T.wait(s_s, mk[("bex", seq, qg, h, sb, tb)])
                            T.op("matmul", pF[0:65, :],
                                 vaug[:, tb, 65 * h:65 * h + 65], expq[tb % 2],
                                 start=(tb == 0), stop=(tb == 15))
                            T.inc_last(s_t)
                            mk[("bav", seq, qg, h, sb, tb)] = s_t.n
                        sc_mm(0); exp_op(0)
                        for tb in range(1, 16):
                            sc_mm(tb)
                            av_mm(tb - 1)
                            exp_op(tb)
                        av_mm(15)
                        gb2 = (4 * sb + h) % 2
                        S.wait(s_t)
                        S.wait(s_os[gb2])
                        S.op("copy", stg[gb2][0:65, :], pF[0:65, :])
                        S.dinc(s_s)
                        Y.wait(s_s)
                        Y.dma(dr["adot_scr"][512 * qg + 64 * h:512 * qg + 64 * h + 64,
                                             512 * sb:512 * sb + 512],
                              stg[gb2][0:64, :], s_os[gb2])
                        Y.dma(dr["sums_scr"][8 * qg + h, 512 * sb:512 * sb + 512]
                              .unsqueeze(0), stg[gb2][64:65, :], s_os[gb2])
            abf.pop(); af32.pop()
            barrier()
            attnb_blend(seq, src_res)

        def attnb_blend(seq, src_res):
            abf.push(); af32.push()
            adot_t = [abf.alloc(D) for _ in range(2)]
            amem_t = [abf.alloc(D) for _ in range(2)]
            sumrows = abf.alloc(N)[0:32, :]
            denrows = abf.alloc(N)[0:16, :]
            t1 = abf.alloc(D)
            t2 = abf.alloc(D)
            rcp = af32.alloc(64)
            Y.dma(sumrows, dr["sums_scr"][:, :], s_sb[0])
            Y.dma(denrows[0:NQ, :], dr["dens_scr"][:, :], s_sb[0])
            T.wait(s_sb[0])
            for lt in range(16):
                b = lt % 2
                gi = seq * 16 + lt
                T.wait(s_v, mk.get(("btp", seq, lt - 1), 0))
                T.op("transpose", pTa[:, 0:32], sumrows[:, 128 * lt:128 * lt + 128],
                     c_id[0:32, 0:32])
                T.op("transpose", pTa[:, 32:48], denrows[0:16, 128 * lt:128 * lt + 128],
                     c_id[0:16, 0:16])
                T.inc_last(s_t)
                V.wait(s_t)
                V.op("tensor_copy", sqs[:, 0:32], pTa[:, 0:32])
                V.op("tensor_copy", sqs[:, 32:36], pTa[:, 32:36])
                V.drain()
                V.op("reciprocal", rcp[:, 0:32], sqs[:, 0:32])
                V.drain()
                V.op("reciprocal", rcp[:, 32:36], sqs[:, 32:36])
                V.drain(inc=(s_v, 1))
                mk[("btp", seq, lt)] = s_v.n
                Y.wait(s_v, mk.get(("bfr", seq, lt - 2), 0))
                Y.wait(s_fm[b])
                Y.op("dma_start", out=adot_t[b],
                     in_=dr["adot_scr"][:, 128 * lt:128 * lt + 128],
                     transpose=True, inc=(s_fm[b], 16))
                Y.op("dma_start", out=amem_t[b],
                     in_=dr["amem_scr"][:, 128 * lt:128 * lt + 128],
                     transpose=True, inc=(s_fm[b], 16))
                Y.wait(s_xt[b])
                Y.dma(xt[b], src_res[128 * gi:128 * gi + 128, :], s_xt[b])
                V.wait(s_fm[b])
                V.wait(s_xt[b])
                V.op("tensor_tensor", t1.rearrange("p (g e) -> p g e", e=64),
                     adot_t[b].rearrange("p (g e) -> p g e", e=64),
                     rcp[:, 0:32].unsqueeze(2).broadcast_to([128, 32, 64]), ALU.mult)
                V.drain()
                V.op("tensor_tensor", t2.rearrange("p (g e) -> p g e", e=512),
                     amem_t[b].rearrange("p (g e) -> p g e", e=512),
                     rcp[:, 32:36].unsqueeze(2).broadcast_to([128, 4, 512]), ALU.mult)
                V.drain()
                V.op("tensor_scalar", t2, t2, wcol[:, 0:1], None, ALU.mult)
                V.drain()
                V.op("scalar_tensor_tensor", sqs, t1, wcol[:, 1:2], t2,
                     ALU.mult, ALU.add)
                V.drain()
                V.wait(s_xw[0])
                V.op("tensor_tensor", xnew[0], sqs, xt[b], ALU.add)
                V.drain(inc=(s_v, 1))
                mk[("bfr", seq, lt)] = s_v.n
                Y.wait(s_v)
                Y.dma(dr["xres_b"][128 * gi:128 * gi + 128, :], xnew[0], s_xw[0])
            Y.wait(s_xw[0])
            abf.pop(); af32.pop()

        prologue(); barrier()
        if "ln1" in stages:
            ln_pass(dr["x"]); barrier()
        if "dmat1" in stages:
            dmat_pass(); barrier()
        if "attna0" in stages:
            attna_seq(0, dr["x"]); barrier()
        if "attna1" in stages:
            attna_seq(1, dr["x"]); barrier()
        if "ln2" in stages:
            ln_pass(dr["xres_a"]); barrier()
        if "dmat2" in stages:
            dmat_pass(); barrier()
        if "ffn1" in stages:
            ffn_pass(dr["xres_a"], dr["xres_b"], "f1"); barrier()
        if "ln3" in stages:
            ln_pass(dr["xres_b"]); barrier()
        if "dmat3" in stages:
            dmat_pass(); barrier()
        if "attnb0" in stages:
            attnb_seq(0, dr["xres_b"]); barrier()
        if "attnb1" in stages:
            attnb_seq(1, dr["xres_b"]); barrier()
        if "ln4" in stages:
            ln_pass(dr["xres_b"]); barrier()
        if "dmat4" in stages:
            dmat_pass(); barrier()
        if "ffn2" in stages:
            ffn_pass(dr["xres_b"], dr["out"], "f2"); barrier()
        kb.emit()
    return nc


def kernel(**inputs):
    nc = build()
    in_maps = [host_inputs(inputs, c) for c in range(NCORE)]
    res = run_bass_kernel_spmd(nc, in_maps, core_ids=list(range(NCORE)))
    return np.concatenate([r["out"].reshape(SEQ, N, D) for r in res.results], axis=0)


# version 05
# speedup vs baseline: 1.3100x; 1.0264x over previous
import numpy as np
import ml_dtypes
import concourse.bass as bass
import concourse.mybir as mybir
from concourse.bass_utils import run_bass_kernel_spmd
from contextlib import ExitStack

F32 = mybir.dt.float32
BF16 = mybir.dt.bfloat16
AF = mybir.ActivationFunctionType
ALU = mybir.AluOpType
AX = mybir.AxisListType
BF = ml_dtypes.bfloat16

NCORE = 8
SEQ = 2
N = 2048
D = 2048
R = SEQ * N
TT = R // 128
TTS = N // 128
NH_A = 16
HD_A = 128
NQ = 4
NH_B = 8
DH_B = 64
DKV = 512
NE = 8
NR = 16

ALL_STAGES = ("ln1", "dmat1", "attna0", "attna1", "ln2", "dmat2", "ffn1",
              "ln3", "dmat3", "attnb0", "attnb1", "ln4", "dmat4", "ffn2")

A_GROUPS = 4
A_HALVES = 2
A_OG = True
A_DELTA = True
A_DCHUNKS = 8
A_PROJLVL = 3
F_LORA = True


class SemC:
    __slots__ = ("h", "n", "name")
    def __init__(self, h, name):
        self.h, self.n, self.name = h, 0, name


class _Op:
    __slots__ = ("kind", "meth", "args", "kwargs", "inc")
    def __init__(self, kind, meth, args, kwargs, inc):
        self.kind, self.meth, self.args, self.kwargs, self.inc = kind, meth, args, kwargs, inc


class Eng:
    def __init__(self, kb, key):
        self.kb, self.key = kb, key

    def op(self, meth, *args, inc=None, **kwargs):
        if inc is not None:
            inc[0].n += inc[1]
        self.kb.plan[self.key].append(_Op("op", meth, args, kwargs, inc))

    def wait(self, sem, val=None):
        v = sem.n if val is None else val
        if v > 0:
            self.kb.plan[self.key].append(_Op("wait", None, (sem, v), {}, None))

    def dma(self, out, in_, inc, **kw):
        self.op("dma_start", out=out, in_=in_, inc=(inc, 16), **kw)

    def drain(self, inc=None):
        self.op("drain", inc=inc)

    def dinc(self, sem, k=1):
        self.inc_last(sem, k)

    def inc_last(self, sem, k=1):
        ops = self.kb.plan[self.key]
        assert ops and ops[-1].kind == "op" and ops[-1].inc is None, "inc_last misuse"
        sem.n += k
        ops[-1].inc = (sem, k)


class Arena:
    def __init__(self, tensor, cols):
        self.t, self.cols, self.top = tensor, cols, 0
        self.marks = []

    def alloc(self, n):
        a = self.top
        assert a + n <= self.cols, f"arena overflow: {a}+{n} > {self.cols}"
        self.top = a + n
        return self.t[:, a:a + n]

    def alloc3(self, n, w):
        return self.alloc(n * w).rearrange("p (n w) -> p n w", w=w)

    def push(self):
        self.marks.append(self.top)

    def pop(self):
        self.top = self.marks.pop()


class Kb:
    def __init__(self, nc):
        self.nc = nc
        self.plan = {k: [] for k in "TVSYG"}
        self.T = Eng(self, "T"); self.V = Eng(self, "V"); self.S = Eng(self, "S")
        self.Y = Eng(self, "Y"); self.G = Eng(self, "G")
        self.sems = []
        self.ctx = None

    def sem(self, name):
        s = SemC(self.ctx.enter_context(self.nc.semaphore(name)), name)
        self.sems.append(s)
        return s

    def sbuf(self, name, shape, dtype=F32):
        return self.ctx.enter_context(self.nc.sbuf_tensor(name, shape, dtype))

    def psum(self, name, shape, dtype=F32):
        return self.ctx.enter_context(self.nc.psum_tensor(name, shape, dtype))

    def emit(self):
        block = self.ctx.enter_context(self.nc.Block())

        def mk(key):
            def body(eng):
                for o in self.plan[key]:
                    if o.kind == "wait":
                        sem, v = o.args
                        eng.wait_ge(sem.h, v)
                    else:
                        ins = getattr(eng, o.meth)(*o.args, **o.kwargs)
                        if o.inc is not None:
                            ins.then_inc(o.inc[0].h, o.inc[1])
            return body
        block.tensor(mk("T"))
        block.vector(mk("V"))
        block.scalar(mk("S"))
        block.sync(mk("Y"))
        block.gpsimd(mk("G"))


def host_inputs(inputs, core):
    x = np.asarray(inputs["L0_attn_a_x"])
    t = lambda a: np.ascontiguousarray(np.asarray(a).T)
    return {
        "x": np.ascontiguousarray(x[2 * core: 2 * core + 2].reshape(R, D)),
        "wqaT": t(inputs["L0_attn_a_Wq"]),
        "wkaT": t(inputs["L0_attn_a_Wk"]),
        "wvaT": t(inputs["L0_attn_a_Wv"]),
        "wbetaT": t(inputs["L0_attn_a_Wbeta"]),
        "wg1T": t(inputs["L0_attn_a_Wg1"]),
        "wg2T": t(inputs["L0_attn_a_Wg2"]),
        "gnorm": np.ascontiguousarray(np.asarray(inputs["L0_attn_a_g_norm"]).reshape(1, D)),
        "woaT": t(inputs["L0_attn_a_Wo"]),
        "wqbT": t(inputs["L0_attn_b_Wq"]),
        "wkbT": t(inputs["L0_attn_b_Wk"]),
        "wvbT": t(inputs["L0_attn_b_Wv"]),
        "mem": np.ascontiguousarray(np.asarray(inputs["L0_attn_b_memory"])),
        "memnorm": np.ascontiguousarray(np.asarray(inputs["L0_attn_b_memory_norm"])),
        "memw": np.ascontiguousarray(np.asarray(inputs["L0_attn_b_memory_weight"]).reshape(1, 1)),
        "wfT": t(inputs["L0_ffn_W"]),
        "fb": np.ascontiguousarray(np.asarray(inputs["L0_ffn_b"]).reshape(1, D)),
        "aTall": np.ascontiguousarray(
            np.asarray(inputs["L0_ffn_lora_A"]).transpose(2, 0, 1).reshape(D, NE * NR)),
        "ball": np.ascontiguousarray(
            np.asarray(inputs["L0_ffn_lora_B"]).transpose(0, 2, 1).reshape(NE * NR, D)),
        "gwT": t(inputs["L0_ffn_gate_w"]),
        "gb": np.ascontiguousarray(np.asarray(inputs["L0_ffn_gate_b"]).reshape(1, NE)),
        "idb": np.eye(128, dtype=np.float32).astype(BF),
        "mask_l": np.tril(np.ones((128, 128), np.float32), -1).astype(BF),
        "mask_u": np.triu(np.ones((128, 128), np.float32), 1).astype(BF),
        "mask_ui": np.triu(np.ones((128, 128), np.float32), 0).astype(BF),
        "onesb": np.ones((1, 512), np.float32).astype(BF),
        "onesf": np.ones((1, 512), np.float32),
        "ccols": np.tile(np.array([[1e-5, 1e-6, -16.0]], np.float32), (128, 1)),
    }


IN_SHAPES = {
    "x": ((R, D), F32), "wqaT": ((D, D), F32), "wkaT": ((D, D), F32),
    "wvaT": ((D, D), F32), "wbetaT": ((D, NH_A), F32), "wg1T": ((D, 128), F32),
    "wg2T": ((128, D), F32), "gnorm": ((1, D), F32), "woaT": ((D, D), F32),
    "wqbT": ((D, D), F32), "wkbT": ((D, DKV), F32), "wvbT": ((D, DKV), F32),
    "mem": ((DKV, DKV), F32), "memnorm": ((DKV, 1), F32), "memw": ((1, 1), F32),
    "wfT": ((D, D), F32), "fb": ((1, D), F32), "aTall": ((D, NE * NR), F32),
    "ball": ((NE * NR, D), F32), "gwT": ((D, NE), F32), "gb": ((1, NE), F32),
    "idb": ((128, 128), BF16),
    "mask_l": ((128, 128), BF16), "mask_u": ((128, 128), BF16),
    "mask_ui": ((128, 128), BF16), "onesb": ((1, 512), BF16), "onesf": ((1, 512), F32),
    "ccols": ((128, 3), F32),
}

SCRATCH = {
    "xn_tm": ((R, D), BF16),
    "xn_fm": ((16, 128, R), BF16),
    "xres_a": ((R, D), F32),
    "xres_b": ((R, D), F32),
    "o_scr": ((NH_A, N, HD_A), BF16),
    "og_tm": ((N, D), BF16),
    "og_fm": ((16, 128, N), BF16),
    "adot_scr": ((D, N), BF16),
    "amem_scr": ((D, N), BF16),
    "sums_scr": ((NQ * NH_B, N), BF16),
    "dens_scr": ((NQ, N), BF16),
    "dbg_scr": ((32, 128, 512), BF16),
}


def build(debug_outs=(), stages=ALL_STAGES):
    stages = set(stages)
    nc = bass.Bass("TRN2", target_bir_lowering=False)
    kb = Kb(nc)
    with ExitStack() as ctx:
        kb.ctx = ctx
        pr_debug = set(debug_outs)
        dr = {}
        for k, (shape, dt) in IN_SHAPES.items():
            dr[k] = nc.dram_tensor(k, list(shape), dt, kind="ExternalInput")
        dr["out"] = nc.dram_tensor("out", [R, D], F32, kind="ExternalOutput")
        for k, (shape, dt) in SCRATCH.items():
            kind = "ExternalOutput" if k in debug_outs else "Internal"
            dr[k] = nc.dram_tensor(k, list(shape), dt, kind=kind)
        T, V, S, Y, G = kb.T, kb.V, kb.S, kb.Y, kb.G

        c_id = kb.sbuf("c_id", [128, 128], BF16)
        c_ml = kb.sbuf("c_ml", [128, 128], BF16)
        c_mu = kb.sbuf("c_mu", [128, 128], BF16)
        c_mui = kb.sbuf("c_mui", [128, 128], BF16)
        c_ones = kb.sbuf("c_ones", [1, 512], BF16)
        c_onesf = kb.sbuf("c_onesf", [1, 512], F32)
        c_cc = kb.sbuf("c_cc", [128, 3], F32)
        nc.const_aps.register(F32, 1e-5, c_cc[:, 0:1])
        nc.const_aps.register(F32, 1e-6, c_cc[:, 1:2])
        nc.const_aps.register(F32, -16.0, c_cc[:, 2:3])
        fbrow = kb.sbuf("fbrow", [1, D], BF16)
        gbrow = kb.sbuf("gbrow", [1, NE], BF16)
        gnrow = kb.sbuf("gnrow", [1, D], BF16)
        memwc = kb.sbuf("memwc", [1, 1], F32)
        wcol = kb.sbuf("wcol", [128, 2], F32)
        colA = kb.sbuf("colA", [128, 16], F32)
        colB = kb.sbuf("colB", [128, 20], F32)
        colC = kb.sbuf("colC", [128, 96], F32)
        gbcast = kb.sbuf("gbcast", [128, D], F32)
        swbeta = kb.sbuf("swbeta", [128, 16, NH_A], BF16)
        swg1 = kb.sbuf("swg1", [128, 16, 128], BF16)
        swg2 = kb.sbuf("swg2", [128, D], BF16)
        sgw = kb.sbuf("sgw", [128, 16, NE], BF16)
        saT = kb.sbuf("saT", [128, 16, NE * NR], BF16)
        sball = kb.sbuf("sball", [128, D], BF16)
        smem = kb.sbuf("smem", [128, 4, DKV], BF16)
        smn = kb.sbuf("smn", [128, 4, 1], BF16)

        ABF_COLS = 55500
        AF32_COLS = 14400
        abf = Arena(kb.sbuf("abf", [128, ABF_COLS], BF16), ABF_COLS)
        af32 = Arena(kb.sbuf("af32", [128, AF32_COLS], F32), AF32_COLS)
        xt = [af32.alloc(D) for _ in range(2)]
        sqs = af32.alloc(D)
        xnew = [af32.alloc(D)]
        xnt = [abf.alloc(D) for _ in range(2)]

        pA = kb.psum("pA", [128, 512], F32)
        pB = kb.psum("pB", [128, 512], F32)
        pC = kb.psum("pC", [128, 512], F32)
        pD = kb.psum("pD", [128, 512], F32)
        pE = kb.psum("pE", [128, 512], F32)
        pF = kb.psum("pF", [128, 512], F32)
        pTa = kb.psum("pTa", [128, 1024], BF16)
        pTb = kb.psum("pTb", [128, 1024], BF16)
        PSF = [pA, pB, pC, pD, pE, pF]

        s_ld = kb.sem("s_ld")
        s_gw = kb.sem("s_gw")
        s_t = kb.sem("s_t")
        s_v = kb.sem("s_v")
        s_s = kb.sem("s_s")
        s_xt = [kb.sem("s_xt0"), kb.sem("s_xt1")]
        s_xn = [kb.sem("s_xn0"), kb.sem("s_xn1")]
        s_xw = [kb.sem("s_xw0"), kb.sem("s_xw1")]
        s_fm = [kb.sem("s_fm0"), kb.sem("s_fm1")]
        s_fs = [kb.sem("s_fs0"), kb.sem("s_fs1")]
        s_sa = [kb.sem("s_sa0"), kb.sem("s_sa1")]
        s_sb = [kb.sem("s_sb0"), kb.sem("s_sb1")]
        s_os = [kb.sem("s_os0"), kb.sem("s_os1")]
        s_pe = kb.sem("s_pe")
        s_ev = kb.sem("s_ev")
        s_dbg = kb.sem("s_dbg")

        def dbg(row, ap, sem):
            if "dbg_scr" not in pr_debug:
                return
            G.wait(sem)
            wdt = ap.shape[-1] if len(ap.shape) == 2 else 512
            npp = ap.shape[0]
            G.dma(dr["dbg_scr"][row, 0:npp, 0:wdt], ap, s_dbg)

        def barrier():
            for e in (T, V, S, Y, G):
                for s in kb.sems:
                    e.wait(s)

        def prologue():
            Y.dma(c_id[:], dr["idb"][:, :], s_ld)
            Y.dma(c_ml[:], dr["mask_l"][:, :], s_ld)
            Y.dma(c_mu[:], dr["mask_u"][:, :], s_ld)
            Y.dma(c_mui[:], dr["mask_ui"][:, :], s_ld)
            Y.dma(c_ones[:], dr["onesb"][:, :], s_ld)
            Y.dma(c_onesf[:], dr["onesf"][:, :], s_ld)
            Y.dma(c_cc[:], dr["ccols"][:, :], s_ld)
            G.dma(gnrow[:], dr["gnorm"][:, :], s_gw)
            Y.dma(memwc[:], dr["memw"][:, :], s_ld)
            G.dma(fbrow[:], dr["fb"][:, :], s_gw)
            G.dma(gbrow[:], dr["gb"][:, :], s_gw)
            G.dma(swbeta[:, :, :], dr["wbetaT"][:, :].rearrange("(c p) h -> p c h", p=128), s_gw)
            G.dma(swg1[:, :, :], dr["wg1T"][:, :].rearrange("(c p) g -> p c g", p=128), s_gw)
            G.dma(swg2[:, :], dr["wg2T"][:, :], s_gw)
            G.dma(sgw[:, :, :], dr["gwT"][:, :].rearrange("(c p) e -> p c e", p=128), s_gw)
            G.dma(saT[:, :, :], dr["aTall"][:, :].rearrange("(c p) e -> p c e", p=128), s_gw)
            G.dma(sball[:, :], dr["ball"][:, :], s_gw)
            G.dma(smem[:, :, :], dr["mem"][:, :].rearrange("(c p) m -> p c m", p=128), s_gw)
            G.dma(smn[:, :, :], dr["memnorm"][:, :].rearrange("(c p) m -> p c m", p=128), s_gw,
                  allow_slow_non_contiguous=True)
            T.wait(s_ld)
            T.wait(s_gw)
            for j in range(4):
                T.op("matmul", pA[:, :], c_ones[0:1, 0:128],
                     gnrow[0:1, 512 * j:512 * j + 512], start=True, stop=True,
                     inc=(s_t, 1))
                V.wait(s_t)
                V.op("tensor_copy", gbcast[:, 512 * j:512 * j + 512], pA[:, :])
                V.drain(inc=(s_v, 1))
                T.wait(s_v)
            S.wait(s_ld)
            S.op("activation", colA[0:1, 0:1], memwc[0:1, 0:1], AF.Sigmoid)
            S.drain(inc=(s_s, 1))
            T.wait(s_s)
            T.op("matmul", pB[:, 0:1], c_onesf[0:1, 0:128], colA[0:1, 0:1],
                 start=True, stop=True, inc=(s_t, 1))
            V.wait(s_t)
            V.op("tensor_copy", wcol[:, 0:1], pB[:, 0:1])
            V.drain()
            V.op("tensor_scalar", wcol[:, 1:2], wcol[:, 0:1], -1.0, 1.0, ALU.mult, ALU.add)
            V.drain(inc=(s_v, 1))
            T.wait(s_v)

        def ln_pass(src_dram):
            af32.push()
            lnb = [xt[0], xt[1]] + [af32.alloc(D) for _ in range(3)]
            apm = {}
            for g in range(TT // 4):
                for j in range(4):
                    i = 4 * g + j
                    Y.wait(s_s, apm.get((i - 5) // 4, 0))
                    Y.dma(lnb[i % 5], src_dram[128 * i:128 * i + 128, :], s_xt[0])
                V.wait(s_xt[0])
                for j in range(4):
                    i = 4 * g + j
                    V.op("scalar_tensor_tensor", sqs, lnb[i % 5], 1.0, lnb[i % 5],
                         ALU.mult, ALU.mult, accum_out=colA[:, 4 + j:5 + j])
                    V.op("tensor_reduce", colA[:, j:j + 1], lnb[i % 5], AX.X, ALU.add)
                V.drain()
                V.op("tensor_scalar", colA[:, 8:12], colA[:, 0:4], 1.0 / D,
                     None, ALU.mult)
                V.drain()
                V.op("tensor_tensor", colA[:, 12:16], colA[:, 8:12], colA[:, 8:12],
                     ALU.mult)
                V.drain()
                V.op("scalar_tensor_tensor", colA[:, 12:16], colA[:, 4:8], 1.0 / D,
                     colA[:, 12:16], ALU.mult, ALU.subtract)
                V.drain(inc=(s_v, 1))
                S.wait(s_v)
                S.op("activation", colB[:, 0:4], colA[:, 12:16], AF.Ln, bias=1e-5)
                S.drain()
                S.op("activation", colB[:, 4:8], colB[:, 0:4], AF.Exp, scale=-0.5)
                S.drain(inc=(s_s, 1))
                V.wait(s_s)
                V.op("scalar_tensor_tensor", colB[:, 8:12], colA[:, 8:12], -1.0,
                     colB[:, 4:8], ALU.mult, ALU.mult)
                V.drain(inc=(s_v, 1))
                S.wait(s_v)
                for j in range(4):
                    i = 4 * g + j
                    S.wait(s_xn[j % 2])
                    S.op("activation", xnt[j % 2], lnb[i % 5], AF.Identity,
                         bias=colB[:, 8 + j:9 + j], scale=colB[:, 4 + j:5 + j])
                    S.dinc(s_s)
                    G.wait(s_s)
                    G.dma(dr["xn_tm"][128 * i:128 * i + 128, :], xnt[j % 2],
                          s_xn[j % 2])
                apm[g] = s_s.n
            Y.wait(s_xn[0]); Y.wait(s_xn[1])
            af32.pop()

        def dmat_pass(src="xn_tm", dst="xn_fm", rows=R):
            abf.push()
            fmb = [abf.alloc(rows) for _ in range(2)]
            for c in range(16):
                b = c % 2
                Y.wait(s_fs[b])
                Y.op("dma_start", out=fmb[b],
                     in_=dr[src][0:rows, 128 * c:128 * c + 128],
                     transpose=True, inc=(s_fm[b], 16))
                G.wait(s_fm[b])
                G.dma(dr[dst][c, :, 0:rows], fmb[b], s_fs[b])
            Y.wait(s_fs[0]); Y.wait(s_fs[1])
            abf.pop()

        mk = {}

        def attna_seq(seq, src_res):
            ts0 = seq * N
            abf.push(); af32.push()
            sxnt = abf.alloc3(16, 128)
            swk = abf.alloc3(16, 512)
            swv = abf.alloc3(16, 512)
            swq = abf.alloc3(16, 512)
            ktm = abf.alloc3(8, 512)
            vtm = abf.alloc3(8, 512)
            qtm = abf.alloc3(8, 512)
            sbeta = af32.alloc(16 * NH_A).rearrange("p (t h) -> p t h", h=NH_A)
            kq = abf.alloc(512)
            sqk = sqs[:, 0:512]
            uP = [af32.alloc(512) for _ in range(2)]
            nP = [af32.alloc(512) for _ in range(2)]
            u1 = [af32.alloc(512) for _ in range(2)]
            n1 = [af32.alloc(512) for _ in range(2)]
            kfm = [abf.alloc(512) for _ in range(2)]
            ktfm = [abf.alloc(512) for _ in range(2)]
            qfm = [abf.alloc(512) for _ in range(2)]
            m2t = [abf.alloc(512) for _ in range(2)]
            ktil = [abf.alloc(512) for _ in range(2)]
            xping = [af32.alloc(512) for _ in range(2)]
            dbf = abf.alloc(512)
            sf32 = af32.alloc(512)
            sbf = abf.alloc(512)
            osg = [abf.alloc(512) for _ in range(2)]

            def proj_tile(g, lt):
                sl = lt % 8
                Y.wait(s_t, mk.get(("asx", seq, g, lt - 1), 0))
                Y.dma(sxnt[:, :, :],
                      dr["xn_fm"][:, :, ts0 + 128 * lt: ts0 + 128 * lt + 128]
                      .rearrange("c p t -> p c t"), s_sa[0])
                T.wait(s_sa[0])
                T.wait(s_ev, mk.get(("akv", seq, g, lt - 1), 0))
                for c in range(16):
                    st, sp = (c == 0), (c == 15)
                    T.op("matmul", pA[:, :], sxnt[:, c, :], swk[:, c, :], start=st, stop=sp)
                    T.op("matmul", pB[:, :], sxnt[:, c, :], swv[:, c, :], start=st, stop=sp)
                    T.op("matmul", pC[:, :], sxnt[:, c, :], swq[:, c, :], start=st, stop=sp)
                    if g == 0:
                        T.op("matmul", pD[:, 0:NH_A], sxnt[:, c, :],
                             swbeta[:, c, :], start=st, stop=sp)
                T.inc_last(s_t)
                mk[("asx", seq, g, lt)] = s_t.n
                S.wait(s_t)
                if A_PROJLVL >= 2:
                    S.op("activation", kq, pA[:, :], AF.Silu)
                    S.op("copy", vtm[:, sl, :], pB[:, :])
                    S.op("activation", qtm[:, sl, :], pC[:, :], AF.Silu)
                    if g == 0:
                        S.op("activation", colC[:, 16:16 + NH_A], pD[:, 0:NH_A], AF.Sigmoid)
                S.drain(inc=(s_s, 1))
                if A_PROJLVL < 3:
                    V.wait(s_s)
                    V.drain(inc=(s_ev, 1))
                    mk[("akv", seq, g, lt)] = s_ev.n
                    return
                if g == 0:
                    V.wait(s_s)
                    V.op("tensor_scalar", sbeta[:, lt, :], colC[:, 16:16 + NH_A],
                         2.0, None, ALU.mult)
                    V.drain()
                V.wait(s_s)
                V.op("tensor_tensor", sqk, kq, kq, ALU.mult)
                V.drain()
                V.op("tensor_reduce", colC[:, 0:4].unsqueeze(2),
                     sqk.rearrange("p (h d) -> p h d", d=128), AX.X, ALU.add)
                V.drain(inc=(s_v, 1))
                S.wait(s_v)
                S.op("activation", colC[:, 4:8], colC[:, 0:4], AF.Ln, bias=1e-6)
                S.drain()
                S.op("activation", colC[:, 8:12], colC[:, 4:8], AF.Exp, scale=-0.5)
                S.drain(inc=(s_s, 1))
                V.wait(s_s)
                V.op("tensor_tensor", ktm[:, sl, :].rearrange("p (h d) -> p h d", d=128),
                     kq.rearrange("p (h d) -> p h d", d=128),
                     colC[:, 8:12].unsqueeze(2).broadcast_to([128, 4, 128]),
                     ALU.mult)
                V.drain(inc=(s_ev, 1))
                mk[("akv", seq, g, lt)] = s_ev.n

            for g in range(A_GROUPS):
                G.wait(s_t)
                G.dma(swk[:, :, :], dr["wkaT"][:, 512 * g:512 * g + 512]
                      .rearrange("(c p) o -> p c o", p=128), s_gw)
                G.dma(swv[:, :, :], dr["wvaT"][:, 512 * g:512 * g + 512]
                      .rearrange("(c p) o -> p c o", p=128), s_gw)
                G.dma(swq[:, :, :], dr["wqaT"][:, 512 * g:512 * g + 512]
                      .rearrange("(c p) o -> p c o", p=128), s_gw)
                T.wait(s_gw)
                for half in range(A_HALVES):
                    T.wait(s_v); T.wait(s_s)
                    for lt in range(8 * half, 8 * half + 8):
                        proj_tile(g, lt)
                    T.wait(s_ev); V.wait(s_ev); S.wait(s_ev)
                    if A_DELTA:
                        delta_group(seq, g, half, ktm, vtm, qtm, sbeta, uP, nP, u1, n1,
                                    kfm, ktfm, qfm, m2t, ktil, xping, dbf, sf32, sbf, osg)
                barrier()
            abf.pop(); af32.pop()
            if A_OG:
                og_phase(seq, src_res)

        def delta_group(seq, g, half, ktm, vtm, qtm, sbeta, uP, nP, u1, n1,
                        kfm, ktfm, qfm, m2t, ktil, xping, dbf, sf32, sbf, osg):
            q3 = lambda ap: ap.rearrange("p (h d) -> p h d", d=128)
            mu_b = c_mu[:, :].unsqueeze(1).broadcast_to([128, 4, 128])
            ml_b = c_ml[:, :].unsqueeze(1).broadcast_to([128, 4, 128])
            mui_b = c_mui[:, :].unsqueeze(1).broadcast_to([128, 4, 128])
            for c in range(8 * half, 8 * half + A_DCHUNKS):
                sl = c % 8
                p = c % 2
                KT = ktm[:, sl, :]
                VT = vtm[:, sl, :]
                QT = qtm[:, sl, :]
                bsl = sbeta[:, c, 4 * g:4 * g + 4].unsqueeze(2).broadcast_to([128, 4, 128])
                V.wait(s_t, mk.get(("dend", seq, g, c - 1), 0))
                V.op("tensor_tensor", q3(ktil[p]), q3(KT), bsl, ALU.mult)
                V.dinc(s_v)
                T.wait(s_v)
                T.wait(s_s, mk.get(("dtp", seq, g, c - 1), 0))
                for h in range(4):
                    hs = slice(128 * h, 128 * h + 128)
                    T.op("transpose", pTa[:, hs], KT[:, hs], c_id[:, :])
                    T.op("transpose", pTa[:, 512 + 128 * h:640 + 128 * h],
                         ktil[p][:, hs], c_id[:, :])
                    T.op("transpose", pTb[:, hs], QT[:, hs], c_id[:, :])
                T.inc_last(s_t)
                S.wait(s_t)
                S.wait(s_t, mk.get(("dend", seq, g, c - 1), 0))
                S.op("copy", kfm[p], pTa[:, 0:512])
                S.op("copy", ktfm[p], pTa[:, 512:1024])
                S.op("copy", qfm[p], pTb[:, 0:512])
                S.dinc(s_s)
                mk[("dtp", seq, g, c)] = s_s.n
                T.wait(s_s)
                T.wait(s_v, mk.get(("dmsk", seq, g, c - 1), 0))
                for h in range(4):
                    hs = slice(128 * h, 128 * h + 128)
                    T.op("matmul", pA[:, hs], kfm[p][:, hs], ktfm[p][:, hs], start=True, stop=True)
                    T.op("matmul", pB[:, hs], ktfm[p][:, hs], kfm[p][:, hs], start=True, stop=True)
                    T.op("matmul", pC[:, hs], kfm[p][:, hs], qfm[p][:, hs], start=True, stop=True)
                T.inc_last(s_t)
                V.wait(s_t)
                V.op("tensor_tensor", q3(u1[p]), q3(pA[:, :]), mu_b, ALU.mult)
                V.op("tensor_tensor", q3(n1[p]), q3(pB[:, :]), ml_b, ALU.mult)
                V.op("tensor_tensor", q3(m2t[p]), q3(pC[:, :]), mui_b, ALU.mult)
                V.dinc(s_v)
                mk[("dmsk", seq, g, c)] = s_v.n

                if c == 0:
                    V.op("tensor_tensor", q3(xping[0]), q3(VT), bsl, ALU.mult)
                    V.drain(inc=(s_v, 1))
                else:
                    T.wait(s_ev)
                    T.wait(s_s, mk[("doev", seq, g, c - 1)] if
                           ("doev", seq, g, c - 1) in mk else 0)
                    for h in range(4):
                        hs = slice(128 * h, 128 * h + 128)
                        T.op("matmul", pF[:, hs], kfm[p][:, hs], sbf[:, hs],
                             start=True, stop=True)
                    T.inc_last(s_t)
                    V.wait(s_t)
                    V.op("tensor_tensor", sqs[:, 0:512], VT, pF[:, :], ALU.subtract)
                    V.drain()
                    V.op("tensor_tensor", q3(xping[0]),
                         sqs[:, 0:512].rearrange("p (h d) -> p h d", d=128),
                         bsl, ALU.mult)
                    V.dinc(s_v)

                xcur = 0
                cur = 0
                first = True
                stages = [(2, 1), (4, 2), (8, 4), (16, 8), (32, 16), (64, 32),
                          (None, 64)]
                for sq_lv, ap_lv in stages:
                    T.wait(s_v)
                    T.wait(s_s)
                    usrc = (u1, n1, c % 2) if ap_lv == 1 else (uP, nP, cur)
                    for h in range(4):
                        hs = slice(128 * h, 128 * h + 128)
                        if sq_lv is not None:
                            T.op("matmul", pD[:, hs], usrc[1][usrc[2]][:, hs],
                                 usrc[0][usrc[2]][:, hs], start=True, stop=True)
                            if sq_lv < 64:
                                T.op("matmul", pE[:, hs], usrc[0][usrc[2]][:, hs],
                                     usrc[1][usrc[2]][:, hs], start=True, stop=True)
                        T.op("matmul", pF[:, hs], usrc[0][usrc[2]][:, hs],
                             xping[xcur][:, hs], start=True, stop=True)
                    T.inc_last(s_t)
                    V.wait(s_t)
                    V.op("tensor_tensor", xping[1 - xcur], xping[xcur], pF[:, :],
                         ALU.subtract if ap_lv == 1 else ALU.add)
                    V.dinc(s_v)
                    xcur = 1 - xcur
                    if sq_lv is not None:
                        S.wait(s_t)
                        nxt = 0 if ap_lv == 1 else 1 - cur
                        S.op("copy", uP[nxt], pD[:, :])
                        if sq_lv < 64:
                            S.op("copy", nP[nxt], pE[:, :])
                        S.dinc(s_s)
                        cur = nxt
                delta = xping[xcur]
                V.op("tensor_copy", dbf, delta)
                V.dinc(s_v)
                T.wait(s_v)
                T.wait(s_s, mk[("doev", seq, g, c - 1)] if
                       ("doev", seq, g, c - 1) in mk else 0)
                for h in range(4):
                    hs = slice(128 * h, 128 * h + 128)
                    if c > 0:
                        T.op("matmul", pF[:, hs], qfm[p][:, hs], sbf[:, hs],
                             start=True, stop=False)
                        T.op("matmul", pF[:, hs], m2t[p][:, hs], dbf[:, hs],
                             start=False, stop=True)
                    else:
                        T.op("matmul", pF[:, hs], m2t[p][:, hs], dbf[:, hs],
                             start=True, stop=True)
                for h in range(4):
                    hs = slice(128 * h, 128 * h + 128)
                    T.op("matmul", pD[:, hs], KT[:, hs], dbf[:, hs], start=True, stop=True)
                T.inc_last(s_t)
                V.wait(s_t)
                if c == 0:
                    V.op("tensor_copy", sf32, pD[:, :])
                else:
                    V.op("tensor_tensor", sf32, sf32, pD[:, :], ALU.add)
                V.drain()
                V.op("tensor_copy", sbf, sf32)
                V.dinc(s_ev)

                b = c % 2
                S.wait(s_t)
                S.wait(s_os[b])
                S.op("copy", osg[b], pF[:, :])
                S.dinc(s_s)
                mk[("doev", seq, g, c)] = s_s.n
                Y.wait(s_s)
                for h in range(4):
                    Y.dma(dr["o_scr"][4 * g + h, 128 * c:128 * c + 128, :],
                          osg[b][:, 128 * h:128 * h + 128], s_os[b])
                mk[("dend", seq, g, c)] = s_t.n

        def og_phase(seq, src_res):
            ts0 = seq * N
            abf.push(); af32.push()
            g1fm = abf.alloc(N)
            gate = abf.alloc(D)
            so_t = [abf.alloc(D) for _ in range(2)]
            og1 = abf.alloc(D)
            og2 = sqs
            xcol = abf.alloc3(16, 512)
            swo = abf.alloc3(16, 512)
            sxg = [abf.alloc3(16, 128) for _ in range(2)]

            for tb in range(4):
                b = tb % 2
                Y.wait(s_t, mk.get(("gxc", seq, tb - 1), 0))
                Y.dma(xcol[:, :, :],
                      dr["xn_fm"][:, :, ts0 + 512 * tb: ts0 + 512 * tb + 512]
                      .rearrange("c p t -> p c t"), s_sb[b])
                T.wait(s_sb[b])
                T.wait(s_s, mk.get(("g1e", seq, tb - 1), 0))
                for c in range(16):
                    T.op("matmul", pA[:, :], swg1[:, c, :], xcol[:, c, :],
                         start=(c == 0), stop=(c == 15))
                T.inc_last(s_t)
                mk[("gxc", seq, tb)] = s_t.n
                S.wait(s_t)
                S.op("copy", g1fm[:, 512 * tb:512 * tb + 512], pA[:, :])
                S.drain(inc=(s_s, 1))
                mk[("g1e", seq, tb)] = s_s.n
            T.wait(s_s)

            for lt in range(16):
                b = lt % 2
                Y.wait(s_v, mk.get(("oso", seq, lt - 2), 0))
                Y.dma(so_t[b].rearrange("p (h d) -> p h d", d=128),
                      dr["o_scr"][:, 128 * lt:128 * lt + 128, :]
                      .rearrange("h t d -> t h d"), s_sb[b])
                T.wait(s_s, mk.get(("ogv", seq, lt - 1), 0))
                for ob in range(4):
                    T.op("matmul", PSF[ob][:, :], g1fm[:, 128 * lt:128 * lt + 128],
                         swg2[:, 512 * ob:512 * ob + 512], start=True, stop=True)
                T.inc_last(s_t)
                S.wait(s_t)
                for ob in range(4):
                    S.op("activation", gate[:, 512 * ob:512 * ob + 512], PSF[ob][:, :],
                         AF.Sigmoid)
                S.drain(inc=(s_s, 1))
                mk[("ogv", seq, lt)] = s_s.n
                V.wait(s_s)
                V.wait(s_sb[b])
                V.op("tensor_tensor", og1, so_t[b], gate, ALU.mult)
                V.drain()
                V.op("tensor_tensor", og2, og1, og1, ALU.mult)
                V.drain()
                V.op("tensor_reduce", colC[:, 0:16].unsqueeze(2),
                     og2.rearrange("p (h d) -> p h d", d=128), AX.X, ALU.add)
                V.drain(inc=(s_v, 1))
                mk[("oso", seq, lt)] = s_v.n
                S.wait(s_v)
                S.op("activation", colC[:, 16:32], colC[:, 0:16], AF.Ln,
                     scale=1.0 / HD_A, bias=1e-6)
                S.drain()
                S.op("activation", colC[:, 16:32], colC[:, 16:32], AF.Exp, scale=-0.5)
                S.drain(inc=(s_s, 1))
                V.wait(s_s)
                V.op("tensor_tensor", og2.rearrange("p (h d) -> p h d", d=128),
                     og1.rearrange("p (h d) -> p h d", d=128),
                     colC[:, 16:32].unsqueeze(2).broadcast_to([128, 16, 128]), ALU.mult)
                V.drain()
                V.wait(s_xn[b])
                V.op("tensor_tensor", xnt[b], og2, gbcast[:, :], ALU.mult)
                V.drain(inc=(s_v, 1))
                Y.wait(s_v)
                Y.dma(dr["og_tm"][128 * lt:128 * lt + 128, :], xnt[b], s_xn[b])
            Y.wait(s_xn[0]); Y.wait(s_xn[1])
            barrier()
            dmat_pass(src="og_tm", dst="og_fm", rows=N)
            barrier()

            for ob in range(4):
                G.wait(s_t, mk.get(("wgw", seq, ob - 1), 0))
                G.dma(swo[:, :, :],
                      dr["woaT"][:, 512 * ob:512 * ob + 512]
                      .rearrange("(c p) o -> p c o", p=128), s_gw)
                T.wait(s_gw)
                T.wait(s_v); T.wait(s_s)
                for lt in range(16):
                    b = lt % 2
                    gi = seq * 16 + lt
                    Y.wait(s_t, mk.get(("wsx", seq, ob, lt - 2), 0))
                    Y.dma(sxg[b][:, :, :], dr["og_fm"][:, :, 128 * lt:128 * lt + 128]
                          .rearrange("c p t -> p c t"), s_sa[b])
                    Y.wait(s_xt[b])
                    Y.dma(xt[b][:, 0:512],
                          src_res[128 * gi:128 * gi + 128, 512 * ob:512 * ob + 512],
                          s_xt[b])
                    T.wait(s_sa[b])
                    T.wait(s_v, mk.get(("wev", seq, ob, lt - 2), 0))
                    pp = PSF[lt % 2]
                    for c in range(16):
                        T.op("matmul", pp[:, :], sxg[b][:, c, :], swo[:, c, :],
                             start=(c == 0), stop=(c == 15))
                    T.inc_last(s_t)
                    mk[("wsx", seq, ob, lt)] = s_t.n
                    V.wait(s_t)
                    V.wait(s_xt[b])
                    V.wait(s_xw[0])
                    V.op("scalar_tensor_tensor", xnew[0][:, 0:512],
                         pp[:, :], 1.0, xt[b][:, 0:512], ALU.mult, ALU.add)
                    V.drain(inc=(s_v, 1))
                    mk[("wev", seq, ob, lt)] = s_v.n
                    Y.wait(s_v)
                    Y.dma(dr["xres_a"][128 * gi:128 * gi + 128,
                                       512 * ob:512 * ob + 512],
                          xnew[0][:, 0:512], s_xw[0])
                mk[("wgw", seq, ob)] = s_t.n
            Y.wait(s_xw[0])
            abf.pop(); af32.pop()

        def ffn_pass(src_res, dst_dram, tag):
            abf.push(); af32.push()
            sxnt = [abf.alloc3(16, 128) for _ in range(4)]
            swf = abf.alloc3(16, 512)
            sxaw = abf.alloc(512)
            xawT = abf.alloc3(TT, 128)
            slog = af32.alloc(32)
            for i4 in range(0, TT, 4):
                Y.wait(s_t, mk.get((tag, "fsx", i4 - 4), 0))
                for j in range(4):
                    i = i4 + j
                    Y.dma(sxnt[j][:, :, :], dr["xn_fm"][:, :, 128 * i:128 * i + 128]
                          .rearrange("c p t -> p c t"), s_sa[j % 2])
                T.wait(s_sa[0]); T.wait(s_sa[1])
                T.wait(s_s, mk.get((tag, "flg", i4 - 4), 0))
                T.wait(s_v, mk.get((tag, "fxa", i4 - 4), 0))
                for j in range(4):
                    for c in range(16):
                        T.op("matmul", pE[:, 8 * j:8 * j + 8], sxnt[j][:, c, :],
                             sgw[:, c, :], start=(c == 0), stop=False)
                    T.op("matmul", pE[:, 8 * j:8 * j + 8], c_ones[0:1, 0:128],
                         gbrow[0:1, :], start=False, stop=True)
                for j in range(4):
                    for c in range(16):
                        T.op("matmul", pF[:, 128 * j:128 * j + 128], sxnt[j][:, c, :],
                             saT[:, c, :], start=(c == 0), stop=(c == 15))
                T.inc_last(s_t)
                mk[(tag, "fsx", i4)] = s_t.n
                S.wait(s_t)
                S.op("copy", slog, pE[:, 0:32])
                S.dinc(s_s)
                mk[(tag, "flg", i4)] = s_s.n
                l3 = slog.rearrange("p (t e) -> p t e", e=NE)
                V.wait(s_s)
                V.op("tensor_reduce", colB[:, 0:4].unsqueeze(2), l3, AX.X, ALU.max)
                V.drain()
                V.op("tensor_tensor", colC[:, 0:32].rearrange("p (t e) -> p t e", e=NE),
                     l3, colB[:, 0:4].unsqueeze(2).broadcast_to([128, 4, NE]), ALU.is_ge)
                V.drain()
                V.op("scalar_tensor_tensor", colC[:, 32:64], colC[:, 0:32],
                     -1e30, slog, ALU.mult, ALU.add)
                V.drain()
                V.op("tensor_reduce", colB[:, 4:8].unsqueeze(2),
                     colC[:, 32:64].rearrange("p (t e) -> p t e", e=NE), AX.X, ALU.max)
                V.drain()
                V.op("tensor_tensor", colC[:, 64:96].rearrange("p (t e) -> p t e", e=NE),
                     colC[:, 32:64].rearrange("p (t e) -> p t e", e=NE),
                     colB[:, 4:8].unsqueeze(2).broadcast_to([128, 4, NE]), ALU.is_ge)
                V.drain()
                V.op("tensor_tensor", colB[:, 8:12], colB[:, 0:4], colB[:, 4:8],
                     ALU.subtract)
                V.drain(inc=(s_v, 1))
                S.wait(s_v)
                S.op("activation", colB[:, 12:16], colB[:, 8:12], AF.Sigmoid)
                S.drain()
                S.op("activation", colB[:, 16:20], colB[:, 8:12], AF.Sigmoid, scale=-1.0)
                S.drain(inc=(s_s, 1))
                V.wait(s_s)
                V.op("tensor_tensor", colC[:, 0:32].rearrange("p (t e) -> p t e", e=NE),
                     colC[:, 0:32].rearrange("p (t e) -> p t e", e=NE),
                     colB[:, 12:16].unsqueeze(2).broadcast_to([128, 4, NE]), ALU.mult)
                V.drain()
                V.op("tensor_tensor", colC[:, 64:96].rearrange("p (t e) -> p t e", e=NE),
                     colC[:, 64:96].rearrange("p (t e) -> p t e", e=NE),
                     colB[:, 16:20].unsqueeze(2).broadcast_to([128, 4, NE]), ALU.mult)
                V.drain()
                V.op("tensor_tensor", colC[:, 0:32], colC[:, 0:32], colC[:, 64:96],
                     ALU.add)
                V.drain()
                V.wait(s_t, mk.get((tag, "ftp", i4 - 4), 0))
                V.op("tensor_tensor", sxaw.rearrange("p (t e r) -> p (t e) r", t=4, e=NE, r=NR),
                     pF[:, 0:512].rearrange("p (t e r) -> p (t e) r", t=4, e=NE, r=NR),
                     colC[:, 0:32].unsqueeze(2).broadcast_to([128, 32, NR]), ALU.mult)
                V.dinc(s_v)
                mk[(tag, "fxa", i4)] = s_v.n
                T.wait(s_v)
                T.wait(s_s, mk.get((tag, "fxt", i4 - 4), 0))
                for j in range(4):
                    T.op("transpose", pTa[:, 128 * j:128 * j + 128],
                         sxaw[:, 128 * j:128 * j + 128], c_id[:, :])
                T.inc_last(s_t)
                mk[(tag, "ftp", i4)] = s_t.n
                S.wait(s_t)
                S.op("copy", xawT[:, i4:i4 + 4, :].rearrange("p n w -> p (n w)"),
                     pTa[:, 0:512])
                S.dinc(s_s)
                mk[(tag, "fxt", i4)] = s_s.n
            T.wait(s_s); V.wait(s_s)
            if "dbg_scr" in pr_debug and tag == "f1":
                for r8 in range(8):
                    dbg(r8, xawT[:, :, :].rearrange("p n w -> p (n w)")[:, 512 * r8:512 * r8 + 512], s_s)
            for ob in range(4):
                G.wait(s_t, mk.get((tag, "fgw", ob - 1), 0))
                G.dma(swf[:, :, :],
                      dr["wfT"][:, 512 * ob:512 * ob + 512]
                      .rearrange("(c p) o -> p c o", p=128), s_gw)
                T.wait(s_gw)
                T.wait(s_v); T.wait(s_s)
                for i in range(TT):
                    b = i % 2
                    Y.wait(s_t, mk.get((tag, "f2x", ob, i - 2),
                                        mk[(tag, "fsx", TT - 4)]))
                    Y.dma(sxnt[b][:, :, :], dr["xn_fm"][:, :, 128 * i:128 * i + 128]
                          .rearrange("c p t -> p c t"), s_sa[b])
                    Y.wait(s_xt[b])
                    Y.dma(xt[b][:, 0:512],
                          src_res[128 * i:128 * i + 128, 512 * ob:512 * ob + 512],
                          s_xt[b])
                    T.wait(s_sa[b])
                    T.wait(s_v, mk.get((tag, "fev", ob, i - 2), 0))
                    pp = PSF[i % 2]
                    for c in range(16):
                        T.op("matmul", pp[:, :], sxnt[b][:, c, :], swf[:, c, :],
                             start=(c == 0), stop=False)
                    T.op("matmul", pp[:, :], c_ones[0:1, 0:128],
                         fbrow[0:1, 512 * ob:512 * ob + 512], start=False, stop=False)
                    if F_LORA:
                        T.op("matmul", pp[:, :], xawT[:, i, :],
                             sball[:, 512 * ob:512 * ob + 512], start=False, stop=True)
                    else:
                        T.op("matmul", pp[:, :], c_ones[0:1, 0:128],
                             fbrow[0:1, 512 * ob:512 * ob + 512], start=False, stop=True)
                    T.inc_last(s_t)
                    mk[(tag, "f2x", ob, i)] = s_t.n
                    V.wait(s_t)
                    V.wait(s_xt[b])
                    V.wait(s_xw[0])
                    V.op("scalar_tensor_tensor", xnew[0][:, 0:512],
                         pp[:, :], 1.0, xt[b][:, 0:512], ALU.mult, ALU.add)
                    V.drain(inc=(s_v, 1))
                    mk[(tag, "fev", ob, i)] = s_v.n
                    Y.wait(s_v)
                    Y.dma(dst_dram[128 * i:128 * i + 128, 512 * ob:512 * ob + 512],
                          xnew[0][:, 0:512], s_xw[0])
                mk[(tag, "fgw", ob)] = s_t.n
            Y.wait(s_xw[0])
            abf.pop(); af32.pop()

        def attnb_seq(seq, src_res):
            ts0 = seq * N
            abf.push(); af32.push()
            kfmB = abf.alloc3(4, 2048)
            vaug = abf.alloc3(16, 8 * 65)
            qfm = abf.alloc3(4, 2048)
            expq = [abf.alloc(512) for _ in range(2)]
            stg = [abf.alloc(512) for _ in range(2)]
            sxnb = abf.alloc3(16, 512)
            shared = abf.alloc(16384)
            swkb = shared[:, 0:8192].rearrange("p (n w) -> p n w", w=512)
            swvb = shared[:, 8192:16384].rearrange("p (n w) -> p n w", w=512)
            swq1 = shared[:, 0:8192].rearrange("p (n w) -> p n w", w=512)
            sqfm = shared[:, 8192:16384].rearrange("p (n w) -> p n w", w=2048)
            G.dma(swkb[:, :, :], dr["wkbT"][:, :].rearrange("(c p) o -> p c o", p=128), s_gw)
            G.dma(swvb[:, :, :], dr["wvbT"][:, :].rearrange("(c p) o -> p c o", p=128), s_gw)
            V.op("memset",
                 vaug[:, :, :].rearrange("p t (h e) -> p t h e", e=65)[:, :, :, 64:65], 1.0)
            V.drain(inc=(s_v, 1))
            T.wait(s_gw)
            T.wait(s_v)

            for tb in range(4):
                b = tb % 2
                Y.wait(s_t, mk.get(("bsx", seq, tb - 1), 0))
                Y.dma(sxnb[:, :, :],
                      dr["xn_fm"][:, :, ts0 + 512 * tb: ts0 + 512 * tb + 512]
                      .rearrange("c p t -> p c t"), s_sb[b])
                T.wait(s_sb[b])
                T.wait(s_s, mk.get(("bkv", seq, tb - 1), 0))
                for ob in range(4):
                    for c in range(16):
                        T.op("matmul", PSF[ob][:, :], swkb[:, c, 128 * ob:128 * ob + 128],
                             sxnb[:, c, :], start=(c == 0), stop=(c == 15))
                for t4 in range(4):
                    T.wait(s_s, mk.get(("bvv", seq, tb, t4 - 2), 0))
                    for c in range(16):
                        T.op("matmul", PSF[4 + (t4 % 2)][:, :],
                             sxnb[:, c, 128 * t4:128 * t4 + 128],
                             swvb[:, c, :], start=(c == 0), stop=(c == 15))
                    T.inc_last(s_t)
                    S.wait(s_t)
                    S.op("copy", vaug[:, 4 * tb + t4, :]
                         .rearrange("p (h e) -> p h e", e=65)[:, :, 0:64],
                         PSF[4 + (t4 % 2)][:, :].rearrange("p (h e) -> p h e", e=64))
                    S.drain(inc=(s_s, 1))
                    mk[("bvv", seq, tb, t4)] = s_s.n
                mk[("bsx", seq, tb)] = s_t.n
                S.wait(s_t)
                for ob in range(4):
                    S.op("copy", kfmB[:, ob, 512 * tb:512 * tb + 512], PSF[ob][:, :])
                S.drain(inc=(s_s, 1))
                mk[("bkv", seq, tb)] = s_s.n
            S.drain(inc=(s_ev, 1))
            T.wait(s_ev)
            V.wait(s_ev)
            mk[("bqw", seq, -1)] = s_t.n

            for qg in range(4):
                G.wait(s_t, mk.get(("bqw", seq, qg - 1), 0))
                G.dma(swq1[:, :, :], dr["wqbT"][:, 512 * qg:512 * qg + 512]
                      .rearrange("(c p) o -> p c o", p=128), s_gw)
                T.wait(s_gw)
                for tb in range(4):
                    b = tb % 2
                    Y.wait(s_t, mk.get(("bqx", seq, qg, tb - 1), 0))
                    Y.dma(sxnb[:, :, :],
                          dr["xn_fm"][:, :, ts0 + 512 * tb: ts0 + 512 * tb + 512]
                          .rearrange("c p t -> p c t"), s_sb[b])
                    T.wait(s_sb[b])
                    T.wait(s_s, mk.get(("bqe", seq, qg, tb - 1), 0))
                    for ob in range(4):
                        for c in range(16):
                            T.op("matmul", PSF[ob][:, :],
                                 swq1[:, c, 128 * ob: 128 * ob + 128],
                                 sxnb[:, c, :], start=(c == 0), stop=(c == 15))
                    T.inc_last(s_t)
                    mk[("bqx", seq, qg, tb)] = s_t.n
                    S.wait(s_t)
                    for ob in range(4):
                        S.op("copy", qfm[:, ob, 512 * tb:512 * tb + 512], PSF[ob][:, :])
                    S.drain(inc=(s_s, 1))
                    mk[("bqe", seq, qg, tb)] = s_s.n
                mk[("bqw", seq, qg)] = s_t.n
                V.wait(s_s)
                for cb in range(4):
                    V.op("tensor_scalar", sqs, qfm[:, cb, :], 0.0, None, ALU.min)
                    V.drain(inc=(s_v, 1))
                    S.wait(s_v)
                    S.op("activation", sqfm[:, cb, :], sqs, AF.Exp)
                    S.drain(inc=(s_s, 1))
                    V.wait(s_s)
                    V.op("tensor_scalar", xnt[0], qfm[:, cb, :], 0.0, None, ALU.max)
                    V.drain()
                    V.op("scalar_tensor_tensor", sqfm[:, cb, :], sqfm[:, cb, :],
                         -1.0, xnt[0], ALU.add, ALU.add)
                    V.drain(inc=(s_v, 1))
                T.wait(s_v)
                for tb in range(4):
                    T.wait(s_s, mk.get(("bme", seq, qg, tb - 1), 0))
                    for ob in range(4):
                        for c in range(4):
                            T.op("matmul", PSF[ob][:, :], smem[:, c, 128 * ob:128 * ob + 128],
                                 sqfm[:, c, 512 * tb:512 * tb + 512],
                                 start=(c == 0), stop=(c == 3))
                    for c in range(4):
                        T.op("matmul", pE[0:1, :], smn[:, c, :],
                             sqfm[:, c, 512 * tb:512 * tb + 512],
                             start=(c == 0), stop=(c == 3))
                    T.inc_last(s_t)
                    S.wait(s_t)
                    for ob in range(4):
                        S.wait(s_os[ob % 2])
                        S.op("copy", stg[ob % 2], PSF[ob][:, :])
                        S.dinc(s_s)
                        Y.wait(s_s)
                        Y.dma(dr["amem_scr"][512 * qg + 128 * ob:512 * qg + 128 * ob + 128,
                                             512 * tb:512 * tb + 512], stg[ob % 2], s_os[ob % 2])
                    S.wait(s_os[0])
                    S.op("copy", stg[0][0:1, :], pE[0:1, :])
                    S.dinc(s_s)
                    Y.wait(s_s)
                    Y.dma(dr["dens_scr"][qg, 512 * tb:512 * tb + 512].unsqueeze(0),
                          stg[0][0:1, :], s_os[0])
                    mk[("bme", seq, qg, tb)] = s_s.n
                S.wait(s_os[0])
                S.drain(inc=(s_ev, 1))
                T.wait(s_ev)
                for h in range(NH_B):
                    ob = h // 2
                    mrow = 64 * (h % 2)
                    for sb in range(4):
                        def sc_mm(tb):
                            T.wait(s_s, mk.get(("bex", seq, qg, h, sb, tb - 2), 0))
                            T.op("matmul", PSF[tb % 2][:, :],
                                 kfmB[mrow:mrow + 64, ob, 128 * tb:128 * tb + 128],
                                 qfm[mrow:mrow + 64, ob, 512 * sb:512 * sb + 512],
                                 start=True, stop=True, inc=(s_t, 1))
                            mk[("bsc", seq, qg, h, sb, tb)] = s_t.n
                        def exp_op(tb):
                            S.wait(s_t, mk[("bsc", seq, qg, h, sb, tb)])
                            S.wait(s_t, mk.get(("bav", seq, qg, h, sb, tb - 2), 0))
                            S.op("activation", expq[tb % 2], PSF[tb % 2][:, :], AF.Exp,
                                 scale=0.125, bias=-16.0)
                            S.dinc(s_s)
                            mk[("bex", seq, qg, h, sb, tb)] = s_s.n
                        def av_mm(tb):
                            T.wait(s_s, mk[("bex", seq, qg, h, sb, tb)])
                            T.op("matmul", pF[0:65, :],
                                 vaug[:, tb, 65 * h:65 * h + 65], expq[tb % 2],
                                 start=(tb == 0), stop=(tb == 15))
                            T.inc_last(s_t)
                            mk[("bav", seq, qg, h, sb, tb)] = s_t.n
                        sc_mm(0); exp_op(0)
                        for tb in range(1, 16):
                            sc_mm(tb)
                            av_mm(tb - 1)
                            exp_op(tb)
                        av_mm(15)
                        gb2 = (4 * sb + h) % 2
                        S.wait(s_t)
                        S.wait(s_os[gb2])
                        S.op("copy", stg[gb2][0:65, :], pF[0:65, :])
                        S.dinc(s_s)
                        Y.wait(s_s)
                        Y.dma(dr["adot_scr"][512 * qg + 64 * h:512 * qg + 64 * h + 64,
                                             512 * sb:512 * sb + 512],
                              stg[gb2][0:64, :], s_os[gb2])
                        Y.dma(dr["sums_scr"][8 * qg + h, 512 * sb:512 * sb + 512]
                              .unsqueeze(0), stg[gb2][64:65, :], s_os[gb2])
            abf.pop(); af32.pop()
            barrier()
            attnb_blend(seq, src_res)

        def attnb_blend(seq, src_res):
            abf.push(); af32.push()
            adot_t = [abf.alloc(D) for _ in range(2)]
            amem_t = [abf.alloc(D) for _ in range(2)]
            sumrows = abf.alloc(N)[0:32, :]
            denrows = abf.alloc(N)[0:16, :]
            t1 = abf.alloc(D)
            t2 = abf.alloc(D)
            rcp = af32.alloc(64)
            Y.dma(sumrows, dr["sums_scr"][:, :], s_sb[0])
            Y.dma(denrows[0:NQ, :], dr["dens_scr"][:, :], s_sb[0])
            T.wait(s_sb[0])
            for lt in range(16):
                b = lt % 2
                gi = seq * 16 + lt
                T.wait(s_v, mk.get(("btp", seq, lt - 1), 0))
                T.op("transpose", pTa[:, 0:32], sumrows[:, 128 * lt:128 * lt + 128],
                     c_id[0:32, 0:32])
                T.op("transpose", pTa[:, 32:48], denrows[0:16, 128 * lt:128 * lt + 128],
                     c_id[0:16, 0:16])
                T.inc_last(s_t)
                V.wait(s_t)
                V.op("tensor_copy", sqs[:, 0:32], pTa[:, 0:32])
                V.op("tensor_copy", sqs[:, 32:36], pTa[:, 32:36])
                V.drain()
                V.op("reciprocal", rcp[:, 0:32], sqs[:, 0:32])
                V.drain()
                V.op("reciprocal", rcp[:, 32:36], sqs[:, 32:36])
                V.drain(inc=(s_v, 1))
                mk[("btp", seq, lt)] = s_v.n
                Y.wait(s_v, mk.get(("bfr", seq, lt - 2), 0))
                Y.wait(s_fm[b])
                Y.op("dma_start", out=adot_t[b],
                     in_=dr["adot_scr"][:, 128 * lt:128 * lt + 128],
                     transpose=True, inc=(s_fm[b], 16))
                Y.op("dma_start", out=amem_t[b],
                     in_=dr["amem_scr"][:, 128 * lt:128 * lt + 128],
                     transpose=True, inc=(s_fm[b], 16))
                Y.wait(s_xt[b])
                Y.dma(xt[b], src_res[128 * gi:128 * gi + 128, :], s_xt[b])
                V.wait(s_fm[b])
                V.wait(s_xt[b])
                V.op("tensor_tensor", t1.rearrange("p (g e) -> p g e", e=64),
                     adot_t[b].rearrange("p (g e) -> p g e", e=64),
                     rcp[:, 0:32].unsqueeze(2).broadcast_to([128, 32, 64]), ALU.mult)
                V.drain()
                V.op("tensor_tensor", t2.rearrange("p (g e) -> p g e", e=512),
                     amem_t[b].rearrange("p (g e) -> p g e", e=512),
                     rcp[:, 32:36].unsqueeze(2).broadcast_to([128, 4, 512]), ALU.mult)
                V.drain()
                V.op("tensor_scalar", t2, t2, wcol[:, 0:1], None, ALU.mult)
                V.drain()
                V.op("scalar_tensor_tensor", sqs, t1, wcol[:, 1:2], t2,
                     ALU.mult, ALU.add)
                V.drain()
                V.wait(s_xw[0])
                V.op("tensor_tensor", xnew[0], sqs, xt[b], ALU.add)
                V.drain(inc=(s_v, 1))
                mk[("bfr", seq, lt)] = s_v.n
                Y.wait(s_v)
                Y.dma(dr["xres_b"][128 * gi:128 * gi + 128, :], xnew[0], s_xw[0])
            Y.wait(s_xw[0])
            abf.pop(); af32.pop()

        prologue(); barrier()
        if "ln1" in stages:
            ln_pass(dr["x"]); barrier()
        if "dmat1" in stages:
            dmat_pass(); barrier()
        if "attna0" in stages:
            attna_seq(0, dr["x"]); barrier()
        if "attna1" in stages:
            attna_seq(1, dr["x"]); barrier()
        if "ln2" in stages:
            ln_pass(dr["xres_a"]); barrier()
        if "dmat2" in stages:
            dmat_pass(); barrier()
        if "ffn1" in stages:
            ffn_pass(dr["xres_a"], dr["xres_b"], "f1"); barrier()
        if "ln3" in stages:
            ln_pass(dr["xres_b"]); barrier()
        if "dmat3" in stages:
            dmat_pass(); barrier()
        if "attnb0" in stages:
            attnb_seq(0, dr["xres_b"]); barrier()
        if "attnb1" in stages:
            attnb_seq(1, dr["xres_b"]); barrier()
        if "ln4" in stages:
            ln_pass(dr["xres_b"]); barrier()
        if "dmat4" in stages:
            dmat_pass(); barrier()
        if "ffn2" in stages:
            ffn_pass(dr["xres_b"], dr["out"], "f2"); barrier()
        kb.emit()
    return nc


def kernel(**inputs):
    nc = build()
    in_maps = [host_inputs(inputs, c) for c in range(NCORE)]
    res = run_bass_kernel_spmd(nc, in_maps, core_ids=list(range(NCORE)))
    return np.concatenate([r["out"].reshape(SEQ, N, D) for r in res.results], axis=0)
